# version 01
# baseline (speedup 1.0000x reference)
import os
import math
import numpy as np
import ml_dtypes
from contextlib import ExitStack

import concourse.bass as bass
import concourse.mybir as mybir
import bass_rust as _bass_rust
from concourse.bass_utils import run_bass_kernel_spmd
from concourse.library_config import all_libraries, standard

F32 = mybir.dt.float32
BF16 = mybir.dt.bfloat16
I16 = mybir.dt.int16
AF = mybir.ActivationFunctionType
ALU = mybir.AluOpType

D = 1024
H = 16
DH = 64
C = 128
S = 4096
NCH = S // C
INTER = 2048
NE = 16384
PH = 4
PN = 32
NK = 128
TOPK = 8


def _host_consts():
    cs = {}
    cs["ident_bf"] = np.eye(128, dtype=ml_dtypes.bfloat16)
    cs["maskT_bf"] = np.triu(np.ones((128, 128), dtype=np.float32)).astype(ml_dtypes.bfloat16)
    cs["iota128"] = np.broadcast_to(np.arange(128, dtype=np.float32), (128, 128)).copy()
    inv_freq = 1.0 / (10000.0 ** (np.arange(0, DH, 2, dtype=np.float64) / DH))
    t = np.arange(S, dtype=np.float64)
    fr = np.einsum("i,j->ij", t, inv_freq)
    emb = np.concatenate([fr, fr], axis=-1)
    cosd = np.cos(emb)
    sind = np.sin(emb)
    sneg = sind.copy()
    sneg[:, :PN] = -sind[:, :PN]
    cs["cosF"] = np.tile(cosd, (1, H)).astype(ml_dtypes.bfloat16)
    cs["sinF"] = np.tile(sneg, (1, H)).astype(ml_dtypes.bfloat16)
    gamma = 1.0 - 2.0 ** (-5.0 - np.arange(H, dtype=np.float64))
    a = np.arange(C, dtype=np.float64)
    gq = np.empty((C, D), dtype=np.float64)
    gk = np.empty((C, D), dtype=np.float64)
    for h in range(H):
        gq[:, h * DH:(h + 1) * DH] = (gamma[h] ** (a + 1.0))[:, None] / 8.0
        gk[:, h * DH:(h + 1) * DH] = (gamma[h] ** (-(a + 1.0)))[:, None]
    cs["gq_bf"] = gq.astype(ml_dtypes.bfloat16)
    cs["gk_bf"] = gk.astype(ml_dtypes.bfloat16)
    gc = np.empty((128, 512), dtype=np.float64)
    for h in range(H):
        gc[(h % 2) * 64:(h % 2) * 64 + 64, (h // 2) * 64:(h // 2) * 64 + 64] = gamma[h] ** C
    cs["gammaC"] = gc.astype(np.float32)
    pic = np.zeros((128, 8, 128), dtype=np.float32)
    for c in range(8):
        for m in range(128):
            pic[c * 16 + m % 16, c, m] = 1.0
    cs["picon"] = pic.reshape(128, 8 * 128)
    return cs


def _prep_weights(inp):
    w = {}
    for nm in ("wq", "wk", "wv", "wg", "wo"):
        w[nm + "T"] = np.ascontiguousarray(inp["L0_attn_" + nm].T)
    w["wquT"] = np.ascontiguousarray(inp["L0_ffn_w_queries"].T)
    w["wgaT"] = np.ascontiguousarray(inp["L0_ffn_w_gate"].T)
    w["wupT"] = np.ascontiguousarray(inp["L0_ffn_w_up"].T)
    w["wdoT"] = np.ascontiguousarray(inp["L0_ffn_w_down"].T)
    k = inp["L0_ffn_keys"]
    kt = np.ascontiguousarray(np.transpose(k, (3, 2, 0, 1)).reshape(PN, 2 * PH * NK))
    w["keysT"] = kt
    return w


def build_nc(n_chunks=NCH, debug=False):
    nc = bass.Bass("TRN2", target_bir_lowering=False, dynamic_dma_scratch_size=16384)
    DT = nc.dram_tensor

    x_in = DT("x_in", [S, D], F32, kind="ExternalInput").ap()
    wT = {}
    for nm in ("wqT", "wkT", "wvT", "wgT", "woT"):
        wT[nm] = DT(nm, [D, D], F32, kind="ExternalInput").ap()
    wT["wquT"] = DT("wquT", [D, 256], F32, kind="ExternalInput").ap()
    wT["wgaT"] = DT("wgaT", [D, INTER], F32, kind="ExternalInput").ap()
    wT["wupT"] = DT("wupT", [D, INTER], F32, kind="ExternalInput").ap()
    wT["wdoT"] = DT("wdoT", [INTER, D], F32, kind="ExternalInput").ap()
    keysT_in = DT("keysT_in", [PN, 1024], F32, kind="ExternalInput").ap()
    de_in = DT("de_in", [NE, D], F32, kind="ExternalInput").ap()
    ue_in = DT("ue_in", [NE, D], F32, kind="ExternalInput").ap()
    c_ident = DT("c_ident", [128, 128], BF16, kind="ExternalInput").ap()
    c_maskT = DT("c_maskT", [128, 128], BF16, kind="ExternalInput").ap()
    c_iota = DT("c_iota", [128, 128], F32, kind="ExternalInput").ap()
    c_cosF = DT("c_cosF", [S, D], BF16, kind="ExternalInput").ap()
    c_sinF = DT("c_sinF", [S, D], BF16, kind="ExternalInput").ap()
    c_gq = DT("c_gq", [C, D], BF16, kind="ExternalInput").ap()
    c_gk = DT("c_gk", [C, D], BF16, kind="ExternalInput").ap()
    c_gammaC = DT("c_gammaC", [128, 512], F32, kind="ExternalInput").ap()
    c_picon = DT("c_picon", [128, 8 * 128], F32, kind="ExternalInput").ap()

    out_d = DT("out", [S, D], F32, kind="ExternalOutput").ap()
    spk = "ExternalOutput" if debug else "Internal"
    xn_tm_sp = DT("xn_tm_sp", [NCH, C, D], BF16, kind=spk)
    xn_fm_sp = DT("xn_fm_sp", [NCH, 128, D], BF16, kind=spk)
    s_sp = DT("s_sp", [NCH, C, D], BF16, kind=spk)
    idxs_sp = DT("idxs_sp", [NCH, 128, 256], I16, kind=spk)
    comb_bf = DT("comb_bf", [NE, 2 * D], BF16, kind="Internal")
    if debug:
        dbg_exp = DT("dbg_exp", [NCH, C, D], F32, kind="ExternalOutput")
        dbg_ffn = DT("dbg_ffn", [NCH, C, D], F32, kind="ExternalOutput")
        dbg_wtil = DT("dbg_wtil", [128, NCH * 32], F32, kind="ExternalOutput")
        dbg_ewraw = DT("dbg_ewraw", [NCH, C, 32], F32, kind="ExternalOutput")
        dbg_gt = DT("dbg_gt", [128, 4 * 2 * D], BF16, kind="ExternalOutput")

    gctx = ExitStack()
    with gctx:
        def sb(name, p, n, dt):
            return gctx.enter_context(nc.sbuf_tensor(name, [p, n], dt))

        def psum(name, p, n, dt=F32):
            return gctx.enter_context(nc.psum_tensor(name, [p, n], dt))

        def sem(name):
            h = gctx.enter_context(nc.semaphore(name))
            return {"h": h, "n": 0}

        ident_sb = sb("ident_sb", 128, 128, BF16)
        wtil_all = sb("wtil_all", 128, NCH * 32, F32)

        PA = psum("PA", 128, 1024)
        PB = psum("PB", 128, 1024)
        PT1 = psum("PT1", 128, 1024, BF16)
        PT2 = psum("PT2", 128, 1024, BF16)
        PS1 = psum("PS1", 128, 512)
        PS2 = psum("PS2", 128, 512)

        LSP = sem("LSP")
        LG = sem("LG")
        GAs = [sem("GA%d" % k) for k in range(8)]
        SPL = sem("SPL")
        ST = sem("ST")
        TE = sem("TE")
        VE = sem("VE")
        SC = sem("SC")

        block = gctx.enter_context(nc.Block())

        def on(engname, fn):
            getattr(block, engname)(fn)

        def bump(s, instr, k=1):
            instr.then_inc(s["h"], k)
            s["n"] += k
            return s["n"]

        m = {}


        actx = ExitStack()
        with actx:
            def asb(name, p, n, dt):
                return actx.enter_context(nc.sbuf_tensor(name, [p, n], dt))

            maskT_sb = asb("maskT_sb", 128, 128, BF16)
            iota_sb = asb("iota_sb", 128, 128, F32)
            gq_sb = asb("gq_sb", 128, D, BF16)
            gk_sb = asb("gk_sb", 128, D, BF16)
            gammaC_sb = asb("gammaC_sb", 128, 512, F32)
            picon_sb = asb("picon_sb", 128, 8 * 128, F32)
            keysT_sb = asb("keysT_sb", PN, 1024, BF16)
            xb = [asb("xb0", 128, D, F32), asb("xb1", 128, D, F32)]
            cosb = [asb("cosb0", 128, D, BF16), asb("cosb1", 128, D, BF16)]
            sinb = [asb("sinb0", 128, D, BF16), asb("sinb1", 128, D, BF16)]
            stateA = asb("stateA", 128, 512, F32)
            stateB = asb("stateB", 128, 512, F32)
            state_bf = asb("state_bf", 128, 512, BF16)
            wq_sb = asb("wq_sb", 128, 8 * 1024, BF16)
            wk_sb = asb("wk_sb", 128, 8 * 1024, BF16)
            wv_sb = asb("wv_sb", 128, 8 * 1024, BF16)
            wg_sb = asb("wg_sb", 128, 8 * 1024, BF16)
            wo_sb = asb("wo_sb", 128, 8 * 1024, BF16)
            wqu_sb = asb("wqu_sb", 128, 8 * 256, BF16)
            ln_dump = asb("ln_dump", 128, D, BF16)
            sumx = asb("sumx", 128, 1, F32)
            sumsq = asb("sumsq", 128, 1, F32)
            negm = asb("negm", 128, 1, F32)
            vpe = asb("vpe", 128, 1, F32)
            rs_t = asb("rs_t", 128, 1, F32)
            rs_t2 = asb("rs_t2", 128, 1, F32)
            nsc1 = asb("nsc1", 128, 1, F32)
            nsc2 = asb("nsc2", 128, 1, F32)
            xn_tm = asb("xn_tm", 128, D, BF16)
            xn_fm = asb("xn_fm", 128, D, BF16)
            rotq = asb("rotq", 128, D, F32)
            rotk = asb("rotk", 128, D, F32)
            t1f = asb("t1f", 128, D, F32)
            t2f = asb("t2f", 128, D, F32)
            qt_tm = asb("qt_tm", 128, D, BF16)
            kt_tm = asb("kt_tm", 128, D, BF16)
            qt_fm = asb("qt_fm", 128, D, BF16)
            kt_fm = asb("kt_fm", 128, D, BF16)
            v_tm = asb("v_tm", 128, D, BF16)
            gsil_tm = asb("gsil_tm", 128, D, BF16)
            sT_sb = [asb("sT0", 128, 128, BF16), asb("sT1", 128, 128, BF16)]
            crossf = asb("crossf", 128, D, F32)
            og_tm = asb("og_tm", 128, D, BF16)
            og_fm = asb("og_fm", 128, D, BF16)
            s_bf = [asb("s_bf0", 128, D, BF16), asb("s_bf1", 128, D, BF16)]
            qpk_sb = asb("qpk_sb", 128, 256, BF16)
            qpk_fm = asb("qpk_fm", PN, 1024, BF16)
            sim_sb = asb("sim_sb", 128, 1024, F32)
            sxy = asb("sxy", 128, 64, F32)
            ixy = asb("ixy", 128, 64, F32)
            eqdump = asb("eqdump", 128, 128, BF16)
            ascore = asb("ascore", 128, 256, F32)
            aidx = asb("aidx", 128, 256, F32)
            scores8 = asb("scores8", 128, 32, F32)
            idx8 = asb("idx8", 128, 32, F32)
            idx8c = asb("idx8c", 128, 32, F32)
            ixs = asb("ixs", 128, 32, F32)
            exp_x = asb("exp_x", 128, 32, F32)
            negmax = asb("negmax", 128, 4, F32)
            denom = asb("denom", 128, 4, F32)
            recipd = asb("recipd", 128, 4, F32)
            wexp = asb("wexp", 128, 32, F32)
            exp_t = asb("exp_t", 128, 32, F32)
            exp_u = asb("exp_u", 128, 32, F32)
            d32 = asb("d32", 128, 32, F32)
            idxs_sb = asb("idxs_sb", 128, 256, I16)

            def su_sp(sp):
                for dst, src in ((ident_sb, c_ident), (maskT_sb, c_maskT),
                                 (iota_sb, c_iota), (gammaC_sb, c_gammaC),
                                 (picon_sb, c_picon)):
                    bump(LSP, sp.dma_start(out=dst[:, :], in_=src), 16)
            on("sync", su_sp)
            m["lsp_consts"] = LSP["n"]

            def su_v(v):
                v.memset(stateA[:, :], 0.0)
                v.memset(state_bf[:, :], 0.0)
                bump(VE, v.memset(wtil_all[:, :], 0.0))
            on("vector", su_v)
            m["ve_init"] = VE["n"]

            def a_gloads(g):
                marks = []
                for dst, src, kchunks, n in (
                        (wq_sb, wT["wqT"], 8, 1024), (wk_sb, wT["wkT"], 8, 1024),
                        (wv_sb, wT["wvT"], 8, 1024), (wg_sb, wT["wgT"], 8, 1024),
                        (wo_sb, wT["woT"], 8, 1024), (wqu_sb, wT["wquT"], 8, 256)):
                    if len(marks) >= 2:
                        g.wait_ge(LG["h"], marks[-2])
                    src_ap = bass.AP(src.tensor if hasattr(src, 'tensor') else src, 0,
                                     [[n, 128], [128 * n, kchunks], [1, n]])
                    marks.append(bump(LG, g.dma_start(out=dst[:, :], in_=src_ap), 16))
                g.wait_ge(LG["h"], marks[-2])
                bump(LG, g.dma_start(out=keysT_sb[:, :], in_=keysT_in), 16)
                bump(LG, g.dma_start(out=gq_sb[:, :], in_=c_gq), 16)
                bump(LG, g.dma_start(out=gk_sb[:, :], in_=c_gk), 16)
            on("gpsimd", a_gloads)
            m["lg_aw"] = LG["n"]

            def conv_g(g):
                marks = []
                for half, src in ((0, de_in), (1, ue_in)):
                    for k in range(8):
                        r0 = k * (NE // 8)
                        if len(marks) >= 2:
                            g.wait_ge(LG["h"], marks[-2])
                        marks.append(bump(LG, g.dma_start(
                            out=comb_bf.ap()[r0:r0 + NE // 8, half * D:(half + 1) * D],
                            in_=src[r0:r0 + NE // 8, :]), 16))
            on("gpsimd", conv_g)
            m["lg_conv"] = LG["n"]

            for c in range(n_chunks):
                par = c % 2

                def a1(sp, c=c, par=par):
                    if c >= 2:
                        sp.wait_ge(VE["h"], m[("ve_lnapply", c - 2)])
                        sp.wait_ge(VE["h"], m[("ve_ropek", c - 2)])
                    bump(LSP, sp.dma_start(out=xb[par][:, :],
                                           in_=x_in[c * C:(c + 1) * C, :]), 16)
                    bump(LSP, sp.dma_start(out=cosb[par][:, :],
                                           in_=c_cosF[c * C:(c + 1) * C, :]), 16)
                    bump(LSP, sp.dma_start(out=sinb[par][:, :],
                                           in_=c_sinF[c * C:(c + 1) * C, :]), 16)
                on("sync", a1)
                m[("lsp_a1", c)] = LSP["n"]

                def a2(s, c=c, par=par):
                    s.wait_ge(LSP["h"], m[("lsp_a1", c)])
                    if c == 0:
                        s.wait_ge(LSP["h"], m["lsp_consts"])
                        s.wait_ge(LG["h"], m["lg_aw"])
                    s.activation(out=ln_dump[:, :], in_=xb[par][:, :], func=AF.Copy,
                                 accum_out=sumx[:, :])
                    bump(SC, s.activation(out=ln_dump[:, :], in_=xb[par][:, :],
                                          func=AF.Square, accum_out=sumsq[:, :]))
                on("scalar", a2)
                m[("sc_ln", c)] = SC["n"]

                def a3(v, c=c, par=par):
                    v.wait_ge(SC["h"], m[("sc_ln", c)])
                    if c >= 1:
                        v.wait_ge(SPL["h"], m[("spl_xn", c - 1)])
                        v.wait_ge(TE["h"], m[("te_xnT", c - 1)])
                    v.tensor_scalar(out=negm[:, :], in0=sumx[:, :], scalar1=-1.0 / D,
                                    scalar2=None, op0=ALU.mult)
                    v.engine_nop()
                    v.tensor_tensor(out=nsc1[:, :], in0=negm[:, :], in1=negm[:, :],
                                    op=ALU.mult)
                    v.tensor_scalar(out=nsc2[:, :], in0=sumsq[:, :], scalar1=1.0 / D,
                                    scalar2=1e-5, op0=ALU.mult, op1=ALU.add)
                    v.engine_nop()
                    v.tensor_tensor(out=vpe[:, :], in0=nsc2[:, :], in1=nsc1[:, :],
                                    op=ALU.subtract)
                    v.engine_nop()
                    v.reciprocal(out=nsc1[:, :], in_=vpe[:, :])
                    v.engine_nop()
                    ra, rb = rs_t, rs_t2
                    v.tensor_scalar(out=ra[:, :], in0=nsc1[:, :], scalar1=0.5,
                                    scalar2=0.5, op0=ALU.mult, op1=ALU.add)
                    v.engine_nop()
                    for _ in range(5):
                        v.tensor_tensor(out=nsc1[:, :], in0=ra[:, :], in1=ra[:, :],
                                        op=ALU.mult)
                        v.engine_nop()
                        v.tensor_tensor(out=nsc2[:, :], in0=nsc1[:, :], in1=vpe[:, :],
                                        op=ALU.mult)
                        v.engine_nop()
                        v.tensor_scalar(out=nsc1[:, :], in0=nsc2[:, :], scalar1=-0.5,
                                        scalar2=1.5, op0=ALU.mult, op1=ALU.add)
                        v.engine_nop()
                        v.tensor_tensor(out=rb[:, :], in0=ra[:, :], in1=nsc1[:, :],
                                        op=ALU.mult)
                        v.engine_nop()
                        ra, rb = rb, ra
                    bump(VE, v.tensor_scalar(out=xn_tm[:, :], in0=xb[par][:, :],
                                             scalar1=negm[:, :], scalar2=ra[:, :],
                                             op0=ALU.add, op1=ALU.mult))
                on("vector", a3)
                m[("ve_lnapply", c)] = VE["n"]

                def a4(t, c=c):
                    t.wait_ge(VE["h"], m[("ve_lnapply", c)])
                    if c >= 1:
                        t.wait_ge(SC["h"], m[("sc_ogT", c - 1)])
                    for j in range(8):
                        ins = t.transpose(PT1[:, j * 128:(j + 1) * 128],
                                          xn_tm[:, j * 128:(j + 1) * 128], ident_sb[:, :])
                    bump(TE, ins)
                on("tensor", a4)
                m[("te_xnT", c)] = TE["n"]

                def a5(s, c=c):
                    s.wait_ge(TE["h"], m[("te_xnT", c)])
                    if c >= 1:
                        s.wait_ge(TE["h"], m[("te_qpk", c - 1)])
                        s.wait_ge(SPL["h"], m[("spl_xn", c - 1)])
                    bump(SC, s.copy(out=xn_fm[:, :], in_=PT1[:, :]))
                on("scalar", a5)
                m[("sc_xnfm", c)] = SC["n"]

                def a6(sp, c=c):
                    sp.wait_ge(SC["h"], m[("sc_xnfm", c)])
                    bump(SPL, sp.dma_start(out=xn_tm_sp.ap()[c], in_=xn_tm[:, :]), 16)
                    bump(SPL, sp.dma_start(out=xn_fm_sp.ap()[c], in_=xn_fm[:, :]), 16)
                on("sync", a6)
                m[("spl_xn", c)] = SPL["n"]

                def a7(t, c=c):
                    t.wait_ge(SC["h"], m[("sc_xnfm", c)])
                    if c >= 1:
                        t.wait_ge(VE["h"], m[("ve_sevac", c - 1)])
                    for ck in range(8):
                        first, last = ck == 0, ck == 7
                        for hh in range(2):
                            t.matmul(PA[:, hh * 512:(hh + 1) * 512],
                                     xn_fm[:, ck * 128:(ck + 1) * 128],
                                     wq_sb[:, ck * 1024 + hh * 512: ck * 1024 + (hh + 1) * 512],
                                     start=first, stop=last)
                        for hh in range(2):
                            ins = t.matmul(PB[:, hh * 512:(hh + 1) * 512],
                                           xn_fm[:, ck * 128:(ck + 1) * 128],
                                           wk_sb[:, ck * 1024 + hh * 512: ck * 1024 + (hh + 1) * 512],
                                           start=first, stop=last)
                    bump(TE, ins)
                on("tensor", a7)
                m[("te_qk", c)] = TE["n"]

                def a8(s, c=c):
                    s.wait_ge(TE["h"], m[("te_qk", c)])
                    ins = None
                    for h in range(H):
                        base = h * DH
                        s.copy(out=rotq[:, base:base + PN], in_=PA[:, base + PN:base + DH])
                        ins = s.copy(out=rotq[:, base + PN:base + DH], in_=PA[:, base:base + PN])
                    bump(SC, ins)
                    for h in range(H):
                        base = h * DH
                        s.copy(out=rotk[:, base:base + PN], in_=PB[:, base + PN:base + DH])
                        ins = s.copy(out=rotk[:, base + PN:base + DH], in_=PB[:, base:base + PN])
                    bump(SC, ins)
                on("scalar", a8)
                m[("sc_rotq", c)] = SC["n"] - 1
                m[("sc_rotk", c)] = SC["n"]

                def a9(v, c=c, par=par):
                    v.wait_ge(SC["h"], m[("sc_rotq", c)])
                    v.tensor_tensor(out=t1f[:, :], in0=PA[:, :], in1=cosb[par][:, :],
                                    op=ALU.mult)
                    v.tensor_tensor(out=t2f[:, :], in0=rotq[:, :], in1=sinb[par][:, :],
                                    op=ALU.mult)
                    v.tensor_tensor(out=t1f[:, :], in0=t1f[:, :], in1=t2f[:, :],
                                    op=ALU.add)
                    bump(VE, v.tensor_tensor(out=qt_tm[:, :], in0=t1f[:, :],
                                             in1=gq_sb[:, :], op=ALU.mult))
                    v.wait_ge(SC["h"], m[("sc_rotk", c)])
                    v.tensor_tensor(out=t1f[:, :], in0=PB[:, :], in1=cosb[par][:, :],
                                    op=ALU.mult)
                    v.tensor_tensor(out=t2f[:, :], in0=rotk[:, :], in1=sinb[par][:, :],
                                    op=ALU.mult)
                    v.tensor_tensor(out=t1f[:, :], in0=t1f[:, :], in1=t2f[:, :],
                                    op=ALU.add)
                    bump(VE, v.tensor_tensor(out=kt_tm[:, :], in0=t1f[:, :],
                                             in1=gk_sb[:, :], op=ALU.mult))
                on("vector", a9)
                m[("ve_ropeq", c)] = VE["n"] - 1
                m[("ve_ropek", c)] = VE["n"]

                def a10(t, c=c):
                    t.wait_ge(VE["h"], m[("ve_ropeq", c)])
                    for ck in range(8):
                        first, last = ck == 0, ck == 7
                        for hh in range(2):
                            ins = t.matmul(PA[:, hh * 512:(hh + 1) * 512],
                                           xn_fm[:, ck * 128:(ck + 1) * 128],
                                           wv_sb[:, ck * 1024 + hh * 512: ck * 1024 + (hh + 1) * 512],
                                           start=first, stop=last)
                    bump(TE, ins)
                    t.wait_ge(VE["h"], m[("ve_ropek", c)])
                    for ck in range(8):
                        first, last = ck == 0, ck == 7
                        for hh in range(2):
                            ins = t.matmul(PB[:, hh * 512:(hh + 1) * 512],
                                           xn_fm[:, ck * 128:(ck + 1) * 128],
                                           wg_sb[:, ck * 1024 + hh * 512: ck * 1024 + (hh + 1) * 512],
                                           start=first, stop=last)
                    bump(TE, ins)
                on("tensor", a10)
                m[("te_v", c)] = TE["n"] - 1
                m[("te_g", c)] = TE["n"]

                def a11(s, c=c):
                    s.wait_ge(TE["h"], m[("te_v", c)])
                    bump(SC, s.copy(out=v_tm[:, :], in_=PA[:, :]))
                    s.wait_ge(TE["h"], m[("te_g", c)])
                    bump(SC, s.activation(out=gsil_tm[:, :], in_=PB[:, :], func=AF.Silu))
                on("scalar", a11)
                m[("sc_v", c)] = SC["n"] - 1
                m[("sc_g", c)] = SC["n"]

                def a12(t, c=c):
                    t.wait_ge(VE["h"], m[("ve_ropeq", c)])
                    if c >= 1:
                        t.wait_ge(SC["h"], m[("sc_qpkT", c - 1)])
                    for j in range(8):
                        ins = t.transpose(PT2[:, j * 128:(j + 1) * 128],
                                          qt_tm[:, j * 128:(j + 1) * 128], ident_sb[:, :])
                    bump(TE, ins)
                    t.wait_ge(VE["h"], m[("ve_ropek", c)])
                    t.wait_ge(SC["h"], m[("sc_xnfm", c)])
                    for j in range(8):
                        ins = t.transpose(PT1[:, j * 128:(j + 1) * 128],
                                          kt_tm[:, j * 128:(j + 1) * 128], ident_sb[:, :])
                    bump(TE, ins)
                on("tensor", a12)
                m[("te_qT", c)] = TE["n"] - 1
                m[("te_kT", c)] = TE["n"]

                def a13(s, c=c):
                    s.wait_ge(TE["h"], m[("te_qT", c)])
                    bump(SC, s.copy(out=qt_fm[:, :], in_=PT2[:, :]))
                    s.wait_ge(TE["h"], m[("te_kT", c)])
                    bump(SC, s.copy(out=kt_fm[:, :], in_=PT1[:, :]))
                on("scalar", a13)
                m[("sc_qT", c)] = SC["n"] - 1
                m[("sc_kT", c)] = SC["n"]

                for h in range(H):
                    pbase = (h % 2) * 64
                    fmcol = (h // 2) * 128
                    PSh = PS1 if h % 2 == 0 else PS2

                    def a14t(t, c=c, h=h, pbase=pbase, fmcol=fmcol, PSh=PSh):
                        if h == 0:
                            t.wait_ge(SC["h"], m[("sc_qT", c)])
                            t.wait_ge(SC["h"], m[("sc_kT", c)])
                            t.wait_ge(SC["h"], m[("sc_v", c)])
                            if c >= 1:
                                t.wait_ge(VE["h"], m[("ve_state", c - 1)])
                                t.wait_ge(TE["h"], m[("te_smm", c - 1)])
                        if h == 1 and c >= 1:
                            t.wait_ge(VE["h"], m[("ve_idxsb", c - 1)])
                        if h >= 2:
                            t.wait_ge(VE["h"], m[("ve_mask", c, h - 2)])
                        bump(TE, t.matmul(PSh[:, 0:128],
                                          kt_fm[pbase:pbase + 64, fmcol:fmcol + 128],
                                          qt_fm[pbase:pbase + 64, fmcol:fmcol + 128],
                                          start=True, stop=True))
                    on("tensor", a14t)
                    m[("te_score", c, h)] = TE["n"]

                    def a14v(v, c=c, h=h, PSh=PSh):
                        v.wait_ge(TE["h"], m[("te_score", c, h)])
                        bump(VE, v.tensor_tensor(out=sT_sb[h % 2][:, :],
                                                 in0=PSh[:, 0:128], in1=maskT_sb[:, :],
                                                 op=ALU.mult))
                    on("vector", a14v)
                    m[("ve_mask", c, h)] = VE["n"]

                    def a14pv(t, c=c, h=h, pbase=pbase, fmcol=fmcol):
                        if h == 0:
                            t.wait_ge(SC["h"], m[("sc_g", c)])
                        t.wait_ge(VE["h"], m[("ve_mask", c, h)])
                        t.matmul(PA[:, h * 64:(h + 1) * 64], sT_sb[h % 2][:, :],
                                 v_tm[:, h * 64:(h + 1) * 64], start=True, stop=True)
                        bump(TE, t.matmul(PB[:, h * 64:(h + 1) * 64],
                                          qt_fm[pbase:pbase + 64, fmcol:fmcol + 128],
                                          state_bf[pbase:pbase + 64,
                                                   (h // 2) * 64:(h // 2) * 64 + 64],
                                          start=True, stop=True))
                    on("tensor", a14pv)
                    m[("te_pv", c, h)] = TE["n"]

                def a15(t, c=c):
                    t.wait_ge(VE["h"], m[("ve_mask", c, 14)])
                    for h in range(H):
                        ins = t.matmul(PS1[(h % 2) * 64:(h % 2) * 64 + 64,
                                           (h // 2) * 64:(h // 2) * 64 + 64],
                                       kt_tm[:, h * 64:(h + 1) * 64],
                                       v_tm[:, h * 64:(h + 1) * 64],
                                       start=True, stop=True)
                    bump(TE, ins)
                on("tensor", a15)
                m[("te_state", c)] = TE["n"]

                def a15v(v, c=c):
                    v.wait_ge(TE["h"], m[("te_state", c)])
                    v.tensor_tensor(out=stateB[:, :], in0=PS1[:, :], in1=stateA[:, :],
                                    op=ALU.add)
                    v.tensor_tensor(out=stateA[:, :], in0=stateB[:, :],
                                    in1=gammaC_sb[:, :], op=ALU.mult)
                    bump(VE, v.tensor_copy(out=state_bf[:, :], in_=stateA[:, :]))
                on("vector", a15v)
                m[("ve_state", c)] = VE["n"]

                def a16(s, c=c):
                    s.wait_ge(TE["h"], m[("te_pv", c, 15)])
                    bump(SC, s.copy(out=crossf[:, :], in_=PB[:, :]))
                on("scalar", a16)
                m[("sc_crossf", c)] = SC["n"]

                def a16v(v, c=c):
                    v.wait_ge(SC["h"], m[("sc_crossf", c)])
                    v.wait_ge(SC["h"], m[("sc_g", c)])
                    v.tensor_tensor(out=t1f[:, :], in0=PA[:, :], in1=crossf[:, :],
                                    op=ALU.add)
                    bump(VE, v.tensor_tensor(out=og_tm[:, :], in0=t1f[:, :],
                                             in1=gsil_tm[:, :], op=ALU.mult))
                on("vector", a16v)
                m[("ve_og", c)] = VE["n"]

                def a17(t, c=c):
                    t.wait_ge(VE["h"], m[("ve_og", c)])
                    for j in range(8):
                        ins = t.transpose(PT1[:, j * 128:(j + 1) * 128],
                                          og_tm[:, j * 128:(j + 1) * 128], ident_sb[:, :])
                    bump(TE, ins)
                on("tensor", a17)
                m[("te_ogT", c)] = TE["n"]

                def a17s(s, c=c):
                    s.wait_ge(TE["h"], m[("te_ogT", c)])
                    bump(SC, s.copy(out=og_fm[:, :], in_=PT1[:, :]))
                on("scalar", a17s)
                m[("sc_ogT", c)] = SC["n"]

                def a18(t, c=c):
                    t.wait_ge(SC["h"], m[("sc_ogT", c)])
                    t.wait_ge(SC["h"], m[("sc_crossf", c)])
                    for ck in range(8):
                        first, last = ck == 0, ck == 7
                        for hh in range(2):
                            ins = t.matmul(PB[:, hh * 512:(hh + 1) * 512],
                                           og_fm[:, ck * 128:(ck + 1) * 128],
                                           wo_sb[:, ck * 1024 + hh * 512: ck * 1024 + (hh + 1) * 512],
                                           start=first, stop=last)
                    bump(TE, ins)
                on("tensor", a18)
                m[("te_smm", c)] = TE["n"]

                def a18v(v, c=c, par=par):
                    v.wait_ge(TE["h"], m[("te_smm", c)])
                    if c >= 2:
                        v.wait_ge(SPL["h"], m[("spl_s", c - 2)])
                    bump(VE, v.tensor_copy(out=s_bf[par][:, :], in_=PB[:, :]))
                on("vector", a18v)
                m[("ve_sevac", c)] = VE["n"]

                def a18sp(sp, c=c, par=par):
                    sp.wait_ge(VE["h"], m[("ve_sevac", c)])
                    bump(SPL, sp.dma_start(out=s_sp.ap()[c], in_=s_bf[par][:, :]), 16)
                on("sync", a18sp)
                m[("spl_s", c)] = SPL["n"]

                def a19(t, c=c):
                    t.wait_ge(VE["h"], m[("ve_state", c)])
                    for ck in range(8):
                        ins = t.matmul(PS2[:, 0:256],
                                       xn_fm[:, ck * 128:(ck + 1) * 128],
                                       wqu_sb[:, ck * 256:(ck + 1) * 256],
                                       start=(ck == 0), stop=(ck == 7))
                    bump(TE, ins)
                on("tensor", a19)
                m[("te_qpk", c)] = TE["n"]

                def a19s(s, c=c):
                    s.wait_ge(TE["h"], m[("te_qpk", c)])
                    bump(SC, s.copy(out=qpk_sb[:, :], in_=PS2[:, 0:256]))
                on("scalar", a19s)
                m[("sc_qpk", c)] = SC["n"]

                def a19t(t, c=c):
                    t.wait_ge(SC["h"], m[("sc_qpk", c)])
                    for j in range(8):
                        ins = t.transpose(PT2[0:PN, j * 128:(j + 1) * 128],
                                          qpk_sb[:, j * 32:(j + 1) * 32], ident_sb[:, :])
                    bump(TE, ins)
                on("tensor", a19t)
                m[("te_qpkT", c)] = TE["n"]

                def a19s2(s, c=c):
                    s.wait_ge(TE["h"], m[("te_qpkT", c)])
                    bump(SC, s.copy(out=qpk_fm[:, :], in_=PT2[0:PN, :]))
                on("scalar", a19s2)
                m[("sc_qpkT", c)] = SC["n"]

                for j in range(8):
                    PSj = PS1 if j % 2 == 0 else PS2

                    def a20t(t, c=c, j=j, PSj=PSj):
                        if j == 0:
                            t.wait_ge(SC["h"], m[("sc_qpkT", c)])
                        if j >= 2:
                            t.wait_ge(SC["h"], m[("sc_sim", c, j - 2)])
                        bump(TE, t.matmul(PSj[:, 128:256],
                                          qpk_fm[:, j * 128:(j + 1) * 128],
                                          keysT_sb[:, j * 128:(j + 1) * 128],
                                          start=True, stop=True))
                    on("tensor", a20t)
                    m[("te_sim", c, j)] = TE["n"]

                    def a20s(s, c=c, j=j, PSj=PSj):
                        s.wait_ge(TE["h"], m[("te_sim", c, j)])
                        bump(SC, s.copy(out=sim_sb[:, j * 128:(j + 1) * 128],
                                        in_=PSj[:, 128:256]))
                    on("scalar", a20s)
                    m[("sc_sim", c, j)] = SC["n"]

                def a21(v, c=c):
                    v.wait_ge(SC["h"], m[("sc_sim", c, 7)])
                    for j in range(8):
                        v.max(out=sxy[:, j * 8:(j + 1) * 8],
                              in_=sim_sb[:, j * 128:(j + 1) * 128])
                    for j in range(8):
                        for r in range(8):
                            v.scalar_tensor_tensor(
                                out=eqdump[:, :],
                                in0=sim_sb[:, j * 128:(j + 1) * 128],
                                scalar=sxy[:, j * 8 + r:j * 8 + r + 1],
                                in1=iota_sb[:, :],
                                op0=ALU.is_equal, op1=ALU.mult,
                                accum_out=ixy[:, j * 8 + r:j * 8 + r + 1])
                    for h in range(PH):
                        for i in range(8):
                            v.tensor_scalar(
                                out=ascore[:, h * 64 + i * 8: h * 64 + (i + 1) * 8],
                                in0=sxy[:, (PH + h) * 8:(PH + h + 1) * 8],
                                scalar1=sxy[:, h * 8 + i:h * 8 + i + 1],
                                scalar2=None, op0=ALU.add)
                    v.tensor_scalar(out=ixs[:, :], in0=ixy[:, 0:32],
                                    scalar1=128.0, scalar2=None, op0=ALU.mult)
                    v.engine_nop()
                    v.engine_nop()
                    for h in range(PH):
                        for i in range(8):
                            v.tensor_scalar(
                                out=aidx[:, h * 64 + i * 8: h * 64 + (i + 1) * 8],
                                in0=ixy[:, (PH + h) * 8:(PH + h + 1) * 8],
                                scalar1=ixs[:, h * 8 + i:h * 8 + i + 1],
                                scalar2=None, op0=ALU.add)
                    v.engine_nop()
                    for h in range(PH):
                        v.max(out=scores8[:, h * 8:(h + 1) * 8],
                              in_=ascore[:, h * 64:(h + 1) * 64])
                    v.engine_nop()
                    v.engine_nop()
                    for h in range(PH):
                        for r in range(8):
                            v.scalar_tensor_tensor(
                                out=eqdump[:, 0:64],
                                in0=ascore[:, h * 64:(h + 1) * 64],
                                scalar=scores8[:, h * 8 + r:h * 8 + r + 1],
                                in1=aidx[:, h * 64:(h + 1) * 64],
                                op0=ALU.is_equal, op1=ALU.mult,
                                accum_out=idx8[:, h * 8 + r:h * 8 + r + 1])
                    v.engine_nop()
                    v.engine_nop()
                    v.tensor_scalar(out=idx8c[:, :], in0=idx8[:, :], scalar1=16383.0,
                                    scalar2=None, op0=ALU.min)
                    for h in range(PH):
                        v.tensor_scalar(out=negmax[:, h:h + 1],
                                        in0=scores8[:, h * 8:h * 8 + 1],
                                        scalar1=-1.0, scalar2=None, op0=ALU.mult)
                    v.engine_nop()
                    v.engine_nop()
                    for h in range(PH):
                        v.tensor_scalar(out=wexp[:, h * 8:(h + 1) * 8],
                                        in0=scores8[:, h * 8:(h + 1) * 8],
                                        scalar1=negmax[:, h:h + 1], scalar2=None,
                                        op0=ALU.add)
                    v.engine_nop()
                    v.engine_nop()
                    v.tensor_scalar(out=exp_x[:, :], in0=wexp[:, :], scalar1=-20.0,
                                    scalar2=1.0 / 16.0, op0=ALU.max, op1=ALU.mult)
                    v.engine_nop()
                    v.engine_nop()
                    v.tensor_scalar(out=exp_t[:, :], in0=exp_x[:, :], scalar1=1.0 / 12.0,
                                    scalar2=1.0, op0=ALU.mult, op1=ALU.add)
                    v.engine_nop()
                    v.engine_nop()
                    for k in range(11, 0, -1):
                        v.tensor_tensor(out=exp_u[:, :], in0=exp_x[:, :], in1=exp_t[:, :],
                                        op=ALU.mult)
                        v.engine_nop()
                        v.engine_nop()
                        v.tensor_scalar(out=exp_t[:, :], in0=exp_u[:, :],
                                        scalar1=1.0 / k, scalar2=1.0,
                                        op0=ALU.mult, op1=ALU.add)
                        v.engine_nop()
                        v.engine_nop()
                    for _ in range(2):
                        v.tensor_tensor(out=exp_u[:, :], in0=exp_t[:, :],
                                        in1=exp_t[:, :], op=ALU.mult)
                        v.engine_nop()
                        v.engine_nop()
                        v.tensor_tensor(out=exp_t[:, :], in0=exp_u[:, :],
                                        in1=exp_u[:, :], op=ALU.mult)
                        v.engine_nop()
                        v.engine_nop()
                    for h in range(PH):
                        v.tensor_scalar(out=d32[:, h * 8:(h + 1) * 8],
                                        in0=exp_t[:, h * 8:(h + 1) * 8], scalar1=1.0,
                                        scalar2=0.0, op0=ALU.mult, op1=ALU.add,
                                        accum_out=denom[:, h:h + 1])
                    v.engine_nop()
                    v.engine_nop()
                    v.reciprocal(out=recipd[:, :], in_=denom[:, :])
                    v.engine_nop()
                    v.engine_nop()
                    for h in range(PH):
                        ins = v.tensor_scalar(
                            out=wtil_all[:, c * 32 + h * 8: c * 32 + (h + 1) * 8],
                            in0=exp_t[:, h * 8:(h + 1) * 8],
                            scalar1=recipd[:, h:h + 1], scalar2=None, op0=ALU.mult)
                    bump(VE, ins)
                on("vector", a21)
                m[("ve_topk", c)] = VE["n"]

                def a22(t, c=c):
                    t.wait_ge(VE["h"], m[("ve_topk", c)])
                    t.wait_ge(SC["h"], m[("sc_sim", c, 7)])
                    for ci in range(8):
                        ins = t.matmul(PS2[:, ci * 32:(ci + 1) * 32],
                                       picon_sb[:, ci * 128:(ci + 1) * 128],
                                       idx8c[:, :], start=True, stop=True)
                    bump(TE, ins)
                on("tensor", a22)
                m[("te_perm", c)] = TE["n"]

                def a22v(v, c=c):
                    v.wait_ge(TE["h"], m[("te_perm", c)])
                    if c >= 1:
                        v.wait_ge(SPL["h"], m[("spl_idx", c - 1)])
                    out_ap = bass.AP(idxs_sb, 0, [[256, 128], [8, 32], [1, 8]])
                    in_ap = bass.AP(PS2, 0, [[512, 128], [1, 32], [32, 8]])
                    bump(VE, v.tensor_copy(out=out_ap, in_=in_ap))
                on("vector", a22v)
                m[("ve_idxsb", c)] = VE["n"]

                def a22sp(sp, c=c):
                    sp.wait_ge(VE["h"], m[("ve_idxsb", c)])
                    bump(SPL, sp.dma_start(out=idxs_sp.ap()[c], in_=idxs_sb[:, :]), 16)
                on("sync", a22sp)
                m[("spl_idx", c)] = SPL["n"]

            m["te_endA"] = TE["n"]
            m["ve_endA"] = VE["n"]
            m["sc_endA"] = SC["n"]

        bctx = ExitStack()
        with bctx:
            def bsb(name, p, n, dt):
                return bctx.enter_context(nc.sbuf_tensor(name, [p, n], dt))

            wga_sb = bsb("wga_sb", 128, 8 * 2048, BF16)
            wup_sb = bsb("wup_sb", 128, 8 * 2048, BF16)
            wdo_sb = bsb("wdo_sb", 128, 16 * 1024, BF16)
            xn_tmB_t = bsb("xn_tmB", 128, D, BF16)
            xn_fmB_t = bsb("xn_fmB", 128, D, BF16)
            s_bfB_t = bsb("s_bfB", 128, D, BF16)
            xbB_t = bsb("xbB", 128, D, F32)
            xn_tmB = [xn_tmB_t, xn_tmB_t]
            xn_fmB = [xn_fmB_t, xn_fmB_t]
            s_bfB = [s_bfB_t, s_bfB_t]
            xbB = [xbB_t, xbB_t]
            idxs_B = [bsb("idxsB0", 128, 256, I16), bsb("idxsB1", 128, 256, I16)]
            gt = [bsb("gt0", 128, 4 * 2048, BF16), bsb("gt1", 128, 4 * 2048, BF16)]
            ewraw = bsb("ewraw", 128, 32, F32)
            ewsil = bsb("ewsil", 128, 32, F32)
            ewf = bsb("ewf", 128, 32, F32)
            eacc = [bsb("eacc0", 128, D, F32), bsb("eacc1", 128, D, F32)]
            g1_bf = bsb("g1_bf", 128, INTER, BF16)
            h1_tm = bsb("h1_tm", 128, INTER, BF16)
            h1_fm = bsb("h1_fm", 128, INTER, BF16)
            ffn_f = bsb("ffn_f", 128, D, F32)
            gsf = bsb("gsf", 128, D, BF16)
            rbuf = bsb("rbuf", 128, D, BF16)
            ostage_t = bsb("ostage", 128, D, F32)
            ostage = [ostage_t, ostage_t]

            nidx_reg = [None]

            def b_gloads(g):
                nidx_reg[0] = g.alloc_register("nidxreg")
                g.reg_mov(nidx_reg[0], 512)
                g.wait_ge(TE["h"], m["te_endA"])
                marks = []
                for dst, src, kchunks, n in ((wga_sb, wT["wgaT"], 8, 2048),
                                             (wup_sb, wT["wupT"], 8, 2048),
                                             (wdo_sb, wT["wdoT"], 16, 1024)):
                    if marks:
                        g.wait_ge(LG["h"], marks[-1])
                    src_ap = bass.AP(src.tensor if hasattr(src, 'tensor') else src, 0,
                                     [[n, 128], [128 * n, kchunks], [1, n]])
                    marks.append(bump(LG, g.dma_start(out=dst[:, :], in_=src_ap), 16))
            on("gpsimd", b_gloads)
            m["lg_bw"] = LG["n"]

            for c in range(n_chunks):
                par = c % 2

                def b1(sp, c=c, par=par):
                    if c == 0:
                        sp.wait_ge(TE["h"], m["te_endA"])
                        sp.wait_ge(VE["h"], m["ve_endA"])
                    if c >= 1:
                        sp.wait_ge(VE["h"], m[("ve_out", c - 1)])
                        sp.wait_ge(TE["h"], m[("te_down", c - 1)])
                    sp.wait_ge(SPL["h"], m[("spl_xn", c)])
                    sp.wait_ge(SPL["h"], m[("spl_s", c)])
                    sp.wait_ge(SPL["h"], m[("spl_idx", c)])
                    bump(LSP, sp.dma_start(out=xn_tmB[par][:, :], in_=xn_tm_sp.ap()[c]), 16)
                    bump(LSP, sp.dma_start(out=xn_fmB[par][:, :], in_=xn_fm_sp.ap()[c]), 16)
                    bump(LSP, sp.dma_start(out=s_bfB[par][:, :], in_=s_sp.ap()[c]), 16)
                    bump(LSP, sp.dma_start(out=xbB[par][:, :],
                                           in_=x_in[c * C:(c + 1) * C, :]), 16)
                    bump(LSP, sp.dma_start(out=idxs_B[par][:, :], in_=idxs_sp.ap()[c]), 16)
                on("sync", b1)
                m[("lsp_b1", c)] = LSP["n"]

                def b3init(v, c=c):
                    if c >= 1:
                        v.wait_ge(VE["h"], m[("ve_ffn", c - 1)])
                        if debug:
                            v.wait_ge(SPL["h"], m[("spl_dbg", c - 1)])
                    bump(VE, v.memset(eacc[0][:, :], 0.0))
                on("vector", b3init)
                m[("ve_eacc0", c)] = VE["n"]

                for gidx in range(8):
                    def b2(g, c=c, gidx=gidx, par=par):
                        if c == 0 and gidx == 0:
                            g.wait_ge(LG["h"], m["lg_conv"])
                            g.wait_ge(LSP["h"], m[("lsp_b1", c)])
                        if gidx == 0:
                            g.wait_ge(LSP["h"], m[("lsp_b1", c)])
                        i_g = c * 8 + gidx
                        if i_g >= 2:
                            pg = i_g - 2
                            g.wait_ge(VE["h"], m[("ve_ue", pg // 8, pg % 8)])
                        if i_g >= 8:
                            ps, pv = m[("ga", (i_g - 8) // 8, (i_g - 8) % 8)]
                            g.wait_ge(GAs[ps]["h"], pv)
                        GAx = GAs[i_g % 8]
                        gt_ap = bass.AP(gt[gidx % 2], 0,
                                        [[4 * 2 * D, 128], [2 * D, 4], [1, 2 * D]])
                        bump(GAx, g.dma_gather(
                            gt_ap, comb_bf.ap(),
                            idxs_B[par][:, gidx * 32:(gidx + 1) * 32],
                            512, nidx_reg[0], 2 * D, elem_step=2 * D), 16)
                    on("gpsimd", b2)
                    m[("ga", c, gidx)] = (c * 8 + gidx) % 8, GAs[(c * 8 + gidx) % 8]["n"]

                    def b3v(v, c=c, gidx=gidx, par=par):
                        gs, gv = m[("ga", c, gidx)]
                        v.wait_ge(GAs[gs]["h"], gv)
                        for j in range(4):
                            ins = v.scalar_tensor_tensor(
                                out=rbuf[:, :],
                                in0=gt[gidx % 2][:, j * 2048:j * 2048 + D],
                                scalar=1.0, in1=xn_tmB[par][:, :],
                                op0=ALU.mult, op1=ALU.mult,
                                accum_out=ewraw[:, gidx * 4 + j:gidx * 4 + j + 1])
                        bump(VE, ins)
                    on("vector", b3v)
                    m[("ve_ew", c, gidx)] = VE["n"]

                    def b3s(s, c=c, gidx=gidx):
                        s.wait_ge(VE["h"], m[("ve_ew", c, gidx)])
                        bump(SC, s.activation(out=ewsil[:, gidx * 4:(gidx + 1) * 4],
                                              in_=ewraw[:, gidx * 4:(gidx + 1) * 4],
                                              func=AF.Silu))
                    on("scalar", b3s)
                    m[("sc_esil", c, gidx)] = SC["n"]

                    def b3v2(v, c=c, gidx=gidx, par=par):
                        v.wait_ge(SC["h"], m[("sc_esil", c, gidx)])
                        v.tensor_tensor(out=ewf[:, gidx * 4:(gidx + 1) * 4],
                                        in0=ewsil[:, gidx * 4:(gidx + 1) * 4],
                                        in1=wtil_all[:, c * 32 + gidx * 4:
                                                     c * 32 + (gidx + 1) * 4],
                                        op=ALU.mult)
                        v.engine_nop()
                        v.engine_nop()
                        for j in range(4):
                            sg = gidx * 4 + j
                            ins = v.scalar_tensor_tensor(
                                out=eacc[(sg + 1) % 2][:, :],
                                in0=gt[gidx % 2][:, j * 2048 + D:(j + 1) * 2048],
                                scalar=ewf[:, sg:sg + 1],
                                in1=eacc[sg % 2][:, :],
                                op0=ALU.mult, op1=ALU.add)
                        bump(VE, ins)
                    on("vector", b3v2)
                    m[("ve_ue", c, gidx)] = VE["n"]

                def b4(t, c=c, par=par):
                    t.wait_ge(LSP["h"], m[("lsp_b1", c)])
                    if c == 0:
                        t.wait_ge(LG["h"], m["lg_bw"])
                    else:
                        t.wait_ge(VE["h"], m[("ve_h1", c - 1)])
                    for ck in range(8):
                        first, last = ck == 0, ck == 7
                        for pc in range(4):
                            PX = PA if pc < 2 else PB
                            ins = t.matmul(
                                PX[:, (pc % 2) * 512:((pc % 2) + 1) * 512],
                                xn_fmB[par][:, ck * 128:(ck + 1) * 128],
                                wga_sb[:, ck * 2048 + pc * 512: ck * 2048 + (pc + 1) * 512],
                                start=first, stop=last)
                    bump(TE, ins)
                on("tensor", b4)
                m[("te_gate", c)] = TE["n"]

                def b4s(s, c=c):
                    s.wait_ge(TE["h"], m[("te_gate", c)])
                    s.activation(out=g1_bf[:, 0:1024], in_=PA[:, :], func=AF.Silu)
                    bump(SC, s.activation(out=g1_bf[:, 1024:2048], in_=PB[:, :],
                                          func=AF.Silu))
                on("scalar", b4s)
                m[("sc_g1", c)] = SC["n"]

                def b5(t, c=c, par=par):
                    t.wait_ge(SC["h"], m[("sc_g1", c)])
                    for ck in range(8):
                        first, last = ck == 0, ck == 7
                        for pc in range(4):
                            PX = PA if pc < 2 else PB
                            ins = t.matmul(
                                PX[:, (pc % 2) * 512:((pc % 2) + 1) * 512],
                                xn_fmB[par][:, ck * 128:(ck + 1) * 128],
                                wup_sb[:, ck * 2048 + pc * 512: ck * 2048 + (pc + 1) * 512],
                                start=first, stop=last)
                    bump(TE, ins)
                on("tensor", b5)
                m[("te_up", c)] = TE["n"]

                def b5v(v, c=c):
                    v.wait_ge(TE["h"], m[("te_up", c)])
                    v.tensor_tensor(out=h1_tm[:, 0:1024], in0=PA[:, :],
                                    in1=g1_bf[:, 0:1024], op=ALU.mult)
                    bump(VE, v.tensor_tensor(out=h1_tm[:, 1024:2048], in0=PB[:, :],
                                             in1=g1_bf[:, 1024:2048], op=ALU.mult))
                on("vector", b5v)
                m[("ve_h1", c)] = VE["n"]

                def b6(t, c=c):
                    t.wait_ge(VE["h"], m[("ve_h1", c)])
                    if c >= 1:
                        t.wait_ge(SC["h"], m[("sc_h1T", c - 1)])
                    for j in range(8):
                        t.transpose(PT1[:, j * 128:(j + 1) * 128],
                                    h1_tm[:, j * 128:(j + 1) * 128], ident_sb[:, :])
                    for j in range(8):
                        ins = t.transpose(PT2[:, j * 128:(j + 1) * 128],
                                          h1_tm[:, 1024 + j * 128:1024 + (j + 1) * 128],
                                          ident_sb[:, :])
                    bump(TE, ins)
                on("tensor", b6)
                m[("te_h1T", c)] = TE["n"]

                def b6s(s, c=c):
                    s.wait_ge(TE["h"], m[("te_h1T", c)])
                    s.copy(out=h1_fm[:, 0:1024], in_=PT1[:, :])
                    bump(SC, s.copy(out=h1_fm[:, 1024:2048], in_=PT2[:, :]))
                on("scalar", b6s)
                m[("sc_h1T", c)] = SC["n"]

                def b7(t, c=c):
                    t.wait_ge(SC["h"], m[("sc_h1T", c)])
                    if c >= 1:
                        t.wait_ge(VE["h"], m[("ve_ffn", c - 1)])
                    for ck in range(16):
                        first, last = ck == 0, ck == 15
                        t.matmul(PS1[:, :], h1_fm[:, ck * 128:(ck + 1) * 128],
                                 wdo_sb[:, ck * 1024:ck * 1024 + 512],
                                 start=first, stop=last)
                        ins = t.matmul(PS2[:, :], h1_fm[:, ck * 128:(ck + 1) * 128],
                                       wdo_sb[:, ck * 1024 + 512:(ck + 1) * 1024],
                                       start=first, stop=last)
                    bump(TE, ins)
                on("tensor", b7)
                m[("te_down", c)] = TE["n"]

                def b8v(v, c=c):
                    v.wait_ge(TE["h"], m[("te_down", c)])
                    v.wait_ge(VE["h"], m[("ve_ue", c, 7)])
                    v.scalar_tensor_tensor(out=ffn_f[:, 0:512], in0=eacc[0][:, 0:512],
                                           scalar=1.0, in1=PS1[:, :],
                                           op0=ALU.mult, op1=ALU.add)
                    bump(VE, v.scalar_tensor_tensor(out=ffn_f[:, 512:1024],
                                                    in0=eacc[0][:, 512:1024],
                                                    scalar=1.0, in1=PS2[:, :],
                                                    op0=ALU.mult, op1=ALU.add))
                on("vector", b8v)
                m[("ve_ffn", c)] = VE["n"]

                def b8s(s, c=c):
                    s.wait_ge(VE["h"], m[("ve_ffn", c)])
                    bump(SC, s.activation(out=gsf[:, :], in_=ffn_f[:, :], func=AF.Silu))
                on("scalar", b8s)
                m[("sc_gsf", c)] = SC["n"]

                def b9v(v, c=c, par=par):
                    v.wait_ge(SC["h"], m[("sc_gsf", c)])
                    if c >= 1:
                        v.wait_ge(ST["h"], m[("st_out", c - 1)])
                    v.tensor_tensor(out=rbuf[:, :], in0=gsf[:, :], in1=s_bfB[par][:, :],
                                    op=ALU.mult)
                    bump(VE, v.scalar_tensor_tensor(out=ostage[par][:, :],
                                                    in0=rbuf[:, :], scalar=1.0,
                                                    in1=xbB[par][:, :],
                                                    op0=ALU.mult, op1=ALU.add))
                on("vector", b9v)
                m[("ve_out", c)] = VE["n"]

                def b9sp(sp, c=c, par=par):
                    sp.wait_ge(VE["h"], m[("ve_out", c)])
                    bump(ST, sp.dma_start(out=out_d[c * C:(c + 1) * C, :],
                                          in_=ostage[par][:, :]), 16)
                on("sync", b9sp)
                m[("st_out", c)] = ST["n"]

                if debug:
                    def bdbg(sp, c=c, par=par):
                        sp.wait_ge(VE["h"], m[("ve_out", c)])
                        bump(SPL, sp.dma_start(out=dbg_exp.ap()[c], in_=eacc[0][:, :]), 16)
                        bump(SPL, sp.dma_start(out=dbg_ffn.ap()[c], in_=ffn_f[:, :]), 16)
                        bump(SPL, sp.dma_start(out=dbg_ewraw.ap()[c], in_=ewraw[:, :]), 16)
                        if c == 0:
                            bump(SPL, sp.dma_start(out=dbg_gt.ap(), in_=gt[0][:, :]), 16)
                    on("sync", bdbg)
                    m[("spl_dbg", c)] = SPL["n"]

            if debug:
                def ddbg(sp):
                    sp.wait_ge(VE["h"], m[("ve_out", n_chunks - 1)])
                    bump(SPL, sp.dma_start(out=dbg_wtil.ap(), in_=wtil_all[:, :]), 16)
                    sp.wait_ge(SPL["h"], SPL["n"])
                on("sync", ddbg)

            def fin(sp, nn=n_chunks):
                sp.wait_ge(ST["h"], m[("st_out", nn - 1)])
            on("sync", fin)

    mmask = {}
    for lib in all_libraries:
        for it in lib.instructions:
            mmask[it] = mmask.get(it, 0) | (1 << lib.index)
    _bass_rust.insert_library_loads(nc, mmask, len(all_libraries), standard.index)
    mybir.codegen_inst_isa_subclasses(nc)
    return nc


_NCH_ENV = int(os.environ.get("MK_NCHUNKS", str(NCH)))
_DEBUG_ENV = os.environ.get("MK_DEBUG", "0") == "1"


def kernel(L0_attn_x, L0_attn_wq, L0_attn_wk, L0_attn_wv, L0_attn_wg, L0_attn_wo,
           L0_ffn_w_queries, L0_ffn_keys, L0_ffn_down_embed, L0_ffn_up_embed,
           L0_ffn_w_gate, L0_ffn_w_up, L0_ffn_w_down):
    inp = dict(L0_attn_x=L0_attn_x, L0_attn_wq=L0_attn_wq, L0_attn_wk=L0_attn_wk,
               L0_attn_wv=L0_attn_wv, L0_attn_wg=L0_attn_wg, L0_attn_wo=L0_attn_wo,
               L0_ffn_w_queries=L0_ffn_w_queries, L0_ffn_keys=L0_ffn_keys,
               L0_ffn_down_embed=L0_ffn_down_embed, L0_ffn_up_embed=L0_ffn_up_embed,
               L0_ffn_w_gate=L0_ffn_w_gate, L0_ffn_w_up=L0_ffn_w_up,
               L0_ffn_w_down=L0_ffn_w_down)
    w = _prep_weights(inp)
    cs = _host_consts()
    nch = _NCH_ENV

    nc = build_nc(n_chunks=nch, debug=_DEBUG_ENV)

    base = {
        "wqT": w["wqT"], "wkT": w["wkT"], "wvT": w["wvT"], "wgT": w["wgT"],
        "woT": w["woT"], "wquT": w["wquT"], "wgaT": w["wgaT"], "wupT": w["wupT"],
        "wdoT": w["wdoT"], "keysT_in": w["keysT"],
        "de_in": np.ascontiguousarray(L0_ffn_down_embed),
        "ue_in": np.ascontiguousarray(L0_ffn_up_embed),
        "c_ident": cs["ident_bf"], "c_maskT": cs["maskT_bf"], "c_iota": cs["iota128"],
        "c_cosF": cs["cosF"], "c_sinF": cs["sinF"], "c_gq": cs["gq_bf"],
        "c_gk": cs["gk_bf"], "c_gammaC": cs["gammaC"], "c_picon": cs["picon"],
    }
    in_maps = []
    for b in range(8):
        im = dict(base)
        im["x_in"] = np.ascontiguousarray(L0_attn_x[b])
        in_maps.append(im)

    res = run_bass_kernel_spmd(nc, in_maps, core_ids=list(range(8)))
    out = np.stack([res.results[b]["out"] for b in range(8)], axis=0)
    return out.astype(np.float32)


# version 06
# speedup vs baseline: 1.0917x; 1.0917x over previous
import os
import math
import numpy as np
import ml_dtypes
from contextlib import ExitStack

import concourse.bass as bass
import concourse.mybir as mybir
import bass_rust as _bass_rust
from concourse.bass_utils import run_bass_kernel_spmd
from concourse.library_config import all_libraries, standard

F32 = mybir.dt.float32
BF16 = mybir.dt.bfloat16
I16 = mybir.dt.int16
AF = mybir.ActivationFunctionType
ALU = mybir.AluOpType

D = 1024
H = 16
DH = 64
C = 128
S = 4096
NCH = S // C
INTER = 2048
NE = 16384
PH = 4
PN = 32
NK = 128
TOPK = 8


def _host_consts():
    cs = {}
    cs["ident_bf"] = np.eye(128, dtype=ml_dtypes.bfloat16)
    cs["maskT_bf"] = np.triu(np.ones((128, 128), dtype=np.float32)).astype(ml_dtypes.bfloat16)
    cs["iota128"] = np.broadcast_to(np.arange(128, dtype=np.float32), (128, 128)).copy()
    inv_freq = 1.0 / (10000.0 ** (np.arange(0, DH, 2, dtype=np.float64) / DH))
    t = np.arange(S, dtype=np.float64)
    fr = np.einsum("i,j->ij", t, inv_freq)
    emb = np.concatenate([fr, fr], axis=-1)
    cosd = np.cos(emb)
    sind = np.sin(emb)
    sneg = sind.copy()
    sneg[:, :PN] = -sind[:, :PN]
    cs["cosF"] = np.tile(cosd, (1, H)).astype(ml_dtypes.bfloat16)
    cs["sinF"] = np.tile(sneg, (1, H)).astype(ml_dtypes.bfloat16)
    gamma = 1.0 - 2.0 ** (-5.0 - np.arange(H, dtype=np.float64))
    a = np.arange(C, dtype=np.float64)
    gq = np.empty((C, D), dtype=np.float64)
    gk = np.empty((C, D), dtype=np.float64)
    for h in range(H):
        gq[:, h * DH:(h + 1) * DH] = (gamma[h] ** (a + 1.0))[:, None] / 8.0
        gk[:, h * DH:(h + 1) * DH] = (gamma[h] ** (-(a + 1.0)))[:, None]
    cs["gq_bf"] = gq.astype(ml_dtypes.bfloat16)
    cs["gk_bf"] = gk.astype(ml_dtypes.bfloat16)
    gc = np.empty((128, 512), dtype=np.float64)
    for h in range(H):
        gc[(h % 2) * 64:(h % 2) * 64 + 64, (h // 2) * 64:(h // 2) * 64 + 64] = gamma[h] ** C
    cs["gammaC"] = gc.astype(np.float32)
    pic = np.zeros((128, 8, 128), dtype=np.float32)
    for c in range(8):
        for m in range(128):
            pic[c * 16 + m % 16, c, m] = 1.0
    cs["picon"] = pic.reshape(128, 8 * 128)
    return cs


def _prep_weights(inp):
    w = {}
    for nm in ("wq", "wk", "wv", "wg", "wo"):
        w[nm + "T"] = np.ascontiguousarray(inp["L0_attn_" + nm].T)
    w["wquT"] = np.ascontiguousarray(inp["L0_ffn_w_queries"].T)
    w["wgaT"] = np.ascontiguousarray(inp["L0_ffn_w_gate"].T)
    w["wupT"] = np.ascontiguousarray(inp["L0_ffn_w_up"].T)
    w["wdoT"] = np.ascontiguousarray(inp["L0_ffn_w_down"].T)
    k = inp["L0_ffn_keys"]
    kt = np.ascontiguousarray(np.transpose(k, (3, 2, 0, 1)).reshape(PN, 2 * PH * NK))
    w["keysT"] = kt
    return w


def build_nc(n_chunks=NCH, debug=False):
    nc = bass.Bass("TRN2", target_bir_lowering=False, dynamic_dma_scratch_size=16384)
    DT = nc.dram_tensor

    x_in = DT("x_in", [S, D], F32, kind="ExternalInput").ap()
    wT = {}
    for nm in ("wqT", "wkT", "wvT", "wgT", "woT"):
        wT[nm] = DT(nm, [D, D], F32, kind="ExternalInput").ap()
    wT["wquT"] = DT("wquT", [D, 256], F32, kind="ExternalInput").ap()
    wT["wgaT"] = DT("wgaT", [D, INTER], F32, kind="ExternalInput").ap()
    wT["wupT"] = DT("wupT", [D, INTER], F32, kind="ExternalInput").ap()
    wT["wdoT"] = DT("wdoT", [INTER, D], F32, kind="ExternalInput").ap()
    keysT_in = DT("keysT_in", [PN, 1024], F32, kind="ExternalInput").ap()
    de_in = DT("de_in", [NE, D], F32, kind="ExternalInput").ap()
    ue_in = DT("ue_in", [NE, D], F32, kind="ExternalInput").ap()
    c_ident = DT("c_ident", [128, 128], BF16, kind="ExternalInput").ap()
    c_maskT = DT("c_maskT", [128, 128], BF16, kind="ExternalInput").ap()
    c_iota = DT("c_iota", [128, 128], F32, kind="ExternalInput").ap()
    c_cosF = DT("c_cosF", [S, D], BF16, kind="ExternalInput").ap()
    c_sinF = DT("c_sinF", [S, D], BF16, kind="ExternalInput").ap()
    c_gq = DT("c_gq", [C, D], BF16, kind="ExternalInput").ap()
    c_gk = DT("c_gk", [C, D], BF16, kind="ExternalInput").ap()
    c_gammaC = DT("c_gammaC", [128, 512], F32, kind="ExternalInput").ap()
    c_picon = DT("c_picon", [128, 8 * 128], F32, kind="ExternalInput").ap()

    out_d = DT("out", [S, D], F32, kind="ExternalOutput").ap()
    spk = "ExternalOutput" if debug else "Internal"
    xn_tm_sp = DT("xn_tm_sp", [NCH, C, D], BF16, kind=spk)
    xn_fm_sp = DT("xn_fm_sp", [NCH, 128, D], BF16, kind=spk)
    s_sp = DT("s_sp", [NCH, C, D], BF16, kind=spk)
    idxs_sp = DT("idxs_sp", [NCH, 128, 256], I16, kind=spk)
    comb_bf = DT("comb_bf", [NE, 2 * D], BF16, kind="Internal")
    if debug:
        dbg_exp = DT("dbg_exp", [NCH, C, D], F32, kind="ExternalOutput")
        dbg_ffn = DT("dbg_ffn", [NCH, C, D], F32, kind="ExternalOutput")
        dbg_wtil = DT("dbg_wtil", [128, NCH * 32], F32, kind="ExternalOutput")
        dbg_ewraw = DT("dbg_ewraw", [NCH, C, 32], F32, kind="ExternalOutput")
        dbg_wtilA = DT("dbg_wtilA", [128, NCH * 32], F32, kind="ExternalOutput")
        dbg_gt = DT("dbg_gt", [128, 4 * 2 * D], BF16, kind="ExternalOutput")

    gctx = ExitStack()
    with gctx:
        def sb(name, p, n, dt):
            return gctx.enter_context(nc.sbuf_tensor(name, [p, n], dt))

        def psum(name, p, n, dt=F32):
            return gctx.enter_context(nc.psum_tensor(name, [p, n], dt))

        def sem(name):
            h = gctx.enter_context(nc.semaphore(name))
            return {"h": h, "n": 0}

        ident_sb = sb("ident_sb", 128, 128, BF16)
        wtil_all = sb("wtil_all", 128, NCH * 32, F32)

        PA = psum("PA", 128, 1024)
        PB = psum("PB", 128, 1024)
        PT1 = psum("PT1", 128, 1024, BF16)
        PT2 = psum("PT2", 128, 1024, BF16)
        PS1 = psum("PS1", 128, 512)
        PS2 = psum("PS2", 128, 512)

        LSP = sem("LSP")
        LG = sem("LG")
        GAs = [sem("GA%d" % k) for k in range(8)]
        SPL = sem("SPL")
        ST = sem("ST")
        TE = sem("TE")
        VE = sem("VE")
        SC = sem("SC")

        block = gctx.enter_context(nc.Block())

        prog = []
        mock_mode = [True]

        class _MockInstr:
            def then_inc(self, *a, **k):
                return self

        class _MockEngine:
            def __getattr__(self, name):
                if name in ("alloc_register",):
                    return lambda *a, **k: None
                return lambda *a, **k: _MockInstr()

        _mock_eng = _MockEngine()

        def on(engname, fn):
            fn(_mock_eng)
            prog.append((engname, fn))

        def bump(s, instr, k=1):
            if not mock_mode[0]:
                instr.then_inc(s["h"], k)
            s["n"] += k
            return s["n"]

        m = {}


        actx = ExitStack()
        with actx:
            def asb(name, p, n, dt):
                return actx.enter_context(nc.sbuf_tensor(name, [p, n], dt))

            maskT_sb = asb("maskT_sb", 128, 128, BF16)
            iota_sb = asb("iota_sb", 128, 128, F32)
            gq_sb = asb("gq_sb", 128, D, BF16)
            gk_sb = asb("gk_sb", 128, D, BF16)
            gammaC_sb = asb("gammaC_sb", 128, 512, F32)
            picon_sb = asb("picon_sb", 128, 8 * 128, F32)
            keysT_sb = asb("keysT_sb", PN, 1024, BF16)
            xb = [asb("xb0", 128, D, F32), asb("xb1", 128, D, F32)]
            cosb = [asb("cosb0", 128, D, BF16), asb("cosb1", 128, D, BF16)]
            sinb = [asb("sinb0", 128, D, BF16), asb("sinb1", 128, D, BF16)]
            stateA = asb("stateA", 128, 512, F32)
            stateB = asb("stateB", 128, 512, F32)
            state_bf = asb("state_bf", 128, 512, BF16)
            wq_sb = asb("wq_sb", 128, 8 * 1024, BF16)
            wk_sb = asb("wk_sb", 128, 8 * 1024, BF16)
            wv_sb = asb("wv_sb", 128, 8 * 1024, BF16)
            wg_sb = asb("wg_sb", 128, 8 * 1024, BF16)
            wo_sb = asb("wo_sb", 128, 8 * 1024, BF16)
            wqu_sb = asb("wqu_sb", 128, 8 * 256, BF16)
            ln_dump = asb("ln_dump", 128, D, BF16)
            sumx = asb("sumx", 128, 1, F32)
            sumsq = asb("sumsq", 128, 1, F32)
            negm = asb("negm", 128, 1, F32)
            vpe = asb("vpe", 128, 1, F32)
            rs_t = asb("rs_t", 128, 1, F32)
            rs_t2 = asb("rs_t2", 128, 1, F32)
            nsc1 = asb("nsc1", 128, 1, F32)
            nsc2 = asb("nsc2", 128, 1, F32)
            xn_tm = asb("xn_tm", 128, D, BF16)
            xn_fm = asb("xn_fm", 128, D, BF16)
            rotq = asb("rotq", 128, D, F32)
            rotk = asb("rotk", 128, D, F32)
            t1f = asb("t1f", 128, D, F32)
            t2f = asb("t2f", 128, D, F32)
            qt_tm = asb("qt_tm", 128, D, BF16)
            kt_tm = asb("kt_tm", 128, D, BF16)
            qt_fm = asb("qt_fm", 128, D, BF16)
            kt_fm = asb("kt_fm", 128, D, BF16)
            v_tm = asb("v_tm", 128, D, BF16)
            gsil_tm = asb("gsil_tm", 128, D, BF16)
            sT_sb = [asb("sT0", 128, 128, BF16), asb("sT1", 128, 128, BF16)]
            crossf = asb("crossf", 128, D, F32)
            og_tm = asb("og_tm", 128, D, BF16)
            og_fm = asb("og_fm", 128, D, BF16)
            s_bf = [asb("s_bf0", 128, D, BF16), asb("s_bf1", 128, D, BF16)]
            qpk_sb = asb("qpk_sb", 128, 256, BF16)
            qpk_fm = asb("qpk_fm", PN, 1024, BF16)
            sim_sb = asb("sim_sb", 128, 1024, F32)
            sxy = asb("sxy", 128, 64, F32)
            ixy = asb("ixy", 128, 64, F32)
            eqdump = asb("eqdump", 128, 128, BF16)
            ascore = asb("ascore", 128, 256, F32)
            aidx = asb("aidx", 128, 256, F32)
            scores8 = asb("scores8", 128, 32, F32)
            idx8 = asb("idx8", 128, 32, F32)
            idx8c = asb("idx8c", 128, 32, F32)
            ixs = asb("ixs", 128, 32, F32)
            exp_x = asb("exp_x", 128, 32, F32)
            negmax = asb("negmax", 128, 4, F32)
            denom = asb("denom", 128, 4, F32)
            recipd = asb("recipd", 128, 4, F32)
            wexp = asb("wexp", 128, 32, F32)
            exp_t = asb("exp_t", 128, 32, F32)
            exp_u = asb("exp_u", 128, 32, F32)
            d32 = asb("d32", 128, 32, F32)
            idxs_sb = asb("idxs_sb", 128, 256, I16)

            def su_sp(sp):
                for dst, src in ((ident_sb, c_ident), (maskT_sb, c_maskT),
                                 (iota_sb, c_iota), (gammaC_sb, c_gammaC),
                                 (picon_sb, c_picon)):
                    bump(LSP, sp.dma_start(out=dst[:, :], in_=src), 16)
            on("sync", su_sp)
            m["lsp_consts"] = LSP["n"]

            def su_v(v):
                v.memset(stateA[:, :], 0.0)
                v.memset(state_bf[:, :], 0.0)
                bump(VE, v.memset(wtil_all[:, :], 0.0))
            on("vector", su_v)
            m["ve_init"] = VE["n"]

            def a_gloads(g):
                marks = []
                for dst, src, kchunks, n in (
                        (wq_sb, wT["wqT"], 8, 1024), (wk_sb, wT["wkT"], 8, 1024),
                        (wv_sb, wT["wvT"], 8, 1024), (wg_sb, wT["wgT"], 8, 1024),
                        (wo_sb, wT["woT"], 8, 1024), (wqu_sb, wT["wquT"], 8, 256)):
                    if len(marks) >= 2:
                        g.wait_ge(LG["h"], marks[-2])
                    src_ap = bass.AP(src.tensor if hasattr(src, 'tensor') else src, 0,
                                     [[n, 128], [128 * n, kchunks], [1, n]])
                    marks.append(bump(LG, g.dma_start(out=dst[:, :], in_=src_ap), 16))
                g.wait_ge(LG["h"], marks[-2])
                bump(LG, g.dma_start(out=keysT_sb[:, :], in_=keysT_in), 16)
                bump(LG, g.dma_start(out=gq_sb[:, :], in_=c_gq), 16)
                bump(LG, g.dma_start(out=gk_sb[:, :], in_=c_gk), 16)
            on("gpsimd", a_gloads)
            m["lg_aw"] = LG["n"]

            def conv_g(g):
                marks = []
                for half, src in ((0, de_in), (1, ue_in)):
                    for k in range(8):
                        r0 = k * (NE // 8)
                        if len(marks) >= 2:
                            g.wait_ge(LG["h"], marks[-2])
                        marks.append(bump(LG, g.dma_start(
                            out=comb_bf.ap()[r0:r0 + NE // 8, half * D:(half + 1) * D],
                            in_=src[r0:r0 + NE // 8, :]), 16))
            on("gpsimd", conv_g)
            m["lg_conv"] = LG["n"]

            for c in range(n_chunks):
                par = c % 2

                def a1(sp, c=c, par=par):
                    if c >= 2:
                        sp.wait_ge(VE["h"], m[("ve_lnapply", c - 2)])
                        sp.wait_ge(VE["h"], m[("ve_ropek", c - 2)])
                    bump(LSP, sp.dma_start(out=xb[par][:, :],
                                           in_=x_in[c * C:(c + 1) * C, :]), 16)
                    bump(LSP, sp.dma_start(out=cosb[par][:, :],
                                           in_=c_cosF[c * C:(c + 1) * C, :]), 16)
                    bump(LSP, sp.dma_start(out=sinb[par][:, :],
                                           in_=c_sinF[c * C:(c + 1) * C, :]), 16)
                on("sync", a1)
                m[("lsp_a1", c)] = LSP["n"]

                def a2(s, c=c, par=par):
                    s.wait_ge(LSP["h"], m[("lsp_a1", c)])
                    if c == 0:
                        s.wait_ge(LSP["h"], m["lsp_consts"])
                        s.wait_ge(LG["h"], m["lg_aw"])
                    s.activation(out=ln_dump[:, :], in_=xb[par][:, :], func=AF.Copy,
                                 accum_out=sumx[:, :])
                    bump(SC, s.activation(out=ln_dump[:, :], in_=xb[par][:, :],
                                          func=AF.Square, accum_out=sumsq[:, :]))
                on("scalar", a2)
                m[("sc_ln", c)] = SC["n"]

                def a3(v, c=c, par=par):
                    v.wait_ge(SC["h"], m[("sc_ln", c)])
                    if c >= 1:
                        v.wait_ge(SPL["h"], m[("spl_xn", c - 1)])
                        v.wait_ge(TE["h"], m[("te_xnT", c - 1)])
                    v.tensor_scalar(out=negm[:, :], in0=sumx[:, :], scalar1=-1.0 / D,
                                    scalar2=None, op0=ALU.mult)
                    v.engine_nop()
                    v.engine_nop()
                    v.engine_nop()
                    v.tensor_tensor(out=nsc1[:, :], in0=negm[:, :], in1=negm[:, :],
                                    op=ALU.mult)
                    v.tensor_scalar(out=nsc2[:, :], in0=sumsq[:, :], scalar1=1.0 / D,
                                    scalar2=1e-5, op0=ALU.mult, op1=ALU.add)
                    v.engine_nop()
                    v.tensor_tensor(out=vpe[:, :], in0=nsc2[:, :], in1=nsc1[:, :],
                                    op=ALU.subtract)
                    v.engine_nop()
                    v.reciprocal(out=nsc1[:, :], in_=vpe[:, :])
                    v.engine_nop()
                    v.engine_nop()
                    v.engine_nop()
                    v.engine_nop()
                    v.engine_nop()
                    v.engine_nop()
                    ra, rb = rs_t, rs_t2
                    v.tensor_scalar(out=ra[:, :], in0=nsc1[:, :], scalar1=0.5,
                                    scalar2=0.5, op0=ALU.mult, op1=ALU.add)
                    v.engine_nop()
                    for _ in range(5):
                        v.tensor_tensor(out=nsc1[:, :], in0=ra[:, :], in1=ra[:, :],
                                        op=ALU.mult)
                        v.engine_nop()
                        v.engine_nop()
                        v.engine_nop()
                        v.tensor_tensor(out=nsc2[:, :], in0=nsc1[:, :], in1=vpe[:, :],
                                        op=ALU.mult)
                        v.engine_nop()
                        v.engine_nop()
                        v.engine_nop()
                        v.tensor_scalar(out=nsc1[:, :], in0=nsc2[:, :], scalar1=-0.5,
                                        scalar2=1.5, op0=ALU.mult, op1=ALU.add)
                        v.engine_nop()
                        v.engine_nop()
                        v.engine_nop()
                        v.tensor_tensor(out=rb[:, :], in0=ra[:, :], in1=nsc1[:, :],
                                        op=ALU.mult)
                        v.engine_nop()
                        v.engine_nop()
                        v.engine_nop()
                        ra, rb = rb, ra
                    bump(VE, v.tensor_scalar(out=xn_tm[:, :], in0=xb[par][:, :],
                                             scalar1=negm[:, :], scalar2=ra[:, :],
                                             op0=ALU.add, op1=ALU.mult))
                on("vector", a3)
                m[("ve_lnapply", c)] = VE["n"]

                def a4(t, c=c):
                    t.wait_ge(VE["h"], m[("ve_lnapply", c)])
                    if c >= 1:
                        t.wait_ge(SC["h"], m[("sc_ogT", c - 1)])
                    for j in range(8):
                        ins = t.transpose(PT1[:, j * 128:(j + 1) * 128],
                                          xn_tm[:, j * 128:(j + 1) * 128], ident_sb[:, :])
                    bump(TE, ins)
                on("tensor", a4)
                m[("te_xnT", c)] = TE["n"]

                def a5(s, c=c):
                    s.wait_ge(TE["h"], m[("te_xnT", c)])
                    if c >= 1:
                        s.wait_ge(TE["h"], m[("te_qpk", c - 1)])
                        s.wait_ge(SPL["h"], m[("spl_xn", c - 1)])
                    bump(SC, s.copy(out=xn_fm[:, :], in_=PT1[:, :]))
                on("scalar", a5)
                m[("sc_xnfm", c)] = SC["n"]

                def a6(sp, c=c):
                    sp.wait_ge(SC["h"], m[("sc_xnfm", c)])
                    bump(SPL, sp.dma_start(out=xn_tm_sp.ap()[c], in_=xn_tm[:, :]), 16)
                    bump(SPL, sp.dma_start(out=xn_fm_sp.ap()[c], in_=xn_fm[:, :]), 16)
                on("sync", a6)
                m[("spl_xn", c)] = SPL["n"]

                def a7(t, c=c):
                    t.wait_ge(SC["h"], m[("sc_xnfm", c)])
                    if c >= 1:
                        t.wait_ge(VE["h"], m[("ve_sevac", c - 1)])
                    for ck in range(8):
                        first, last = ck == 0, ck == 7
                        for hh in range(2):
                            t.matmul(PA[:, hh * 512:(hh + 1) * 512],
                                     xn_fm[:, ck * 128:(ck + 1) * 128],
                                     wq_sb[:, ck * 1024 + hh * 512: ck * 1024 + (hh + 1) * 512],
                                     start=first, stop=last)
                        for hh in range(2):
                            ins = t.matmul(PB[:, hh * 512:(hh + 1) * 512],
                                           xn_fm[:, ck * 128:(ck + 1) * 128],
                                           wk_sb[:, ck * 1024 + hh * 512: ck * 1024 + (hh + 1) * 512],
                                           start=first, stop=last)
                    bump(TE, ins)
                on("tensor", a7)
                m[("te_qk", c)] = TE["n"]

                def a8(s, c=c):
                    s.wait_ge(TE["h"], m[("te_qk", c)])
                    lo_q = bass.AP(rotq, 0, [[D, 128], [DH, H], [1, PN]])
                    hi_q = bass.AP(rotq, PN, [[D, 128], [DH, H], [1, PN]])
                    lo_pa = bass.AP(PA, 0, [[1024, 128], [DH, H], [1, PN]])
                    hi_pa = bass.AP(PA, PN, [[1024, 128], [DH, H], [1, PN]])
                    s.copy(out=lo_q, in_=hi_pa)
                    bump(SC, s.copy(out=hi_q, in_=lo_pa))
                    lo_k = bass.AP(rotk, 0, [[D, 128], [DH, H], [1, PN]])
                    hi_k = bass.AP(rotk, PN, [[D, 128], [DH, H], [1, PN]])
                    lo_pb = bass.AP(PB, 0, [[1024, 128], [DH, H], [1, PN]])
                    hi_pb = bass.AP(PB, PN, [[1024, 128], [DH, H], [1, PN]])
                    s.copy(out=lo_k, in_=hi_pb)
                    bump(SC, s.copy(out=hi_k, in_=lo_pb))
                on("scalar", a8)
                m[("sc_rotq", c)] = SC["n"] - 1
                m[("sc_rotk", c)] = SC["n"]

                def a9(v, c=c, par=par):
                    v.wait_ge(SC["h"], m[("sc_rotq", c)])
                    v.tensor_tensor(out=t1f[:, :], in0=PA[:, :], in1=cosb[par][:, :],
                                    op=ALU.mult)
                    v.tensor_tensor(out=t2f[:, :], in0=rotq[:, :], in1=sinb[par][:, :],
                                    op=ALU.mult)
                    v.tensor_tensor(out=t1f[:, :], in0=t1f[:, :], in1=t2f[:, :],
                                    op=ALU.add)
                    bump(VE, v.tensor_tensor(out=qt_tm[:, :], in0=t1f[:, :],
                                             in1=gq_sb[:, :], op=ALU.mult))
                    v.wait_ge(SC["h"], m[("sc_rotk", c)])
                    v.tensor_tensor(out=t1f[:, :], in0=PB[:, :], in1=cosb[par][:, :],
                                    op=ALU.mult)
                    v.tensor_tensor(out=t2f[:, :], in0=rotk[:, :], in1=sinb[par][:, :],
                                    op=ALU.mult)
                    v.tensor_tensor(out=t1f[:, :], in0=t1f[:, :], in1=t2f[:, :],
                                    op=ALU.add)
                    bump(VE, v.tensor_tensor(out=kt_tm[:, :], in0=t1f[:, :],
                                             in1=gk_sb[:, :], op=ALU.mult))
                on("vector", a9)
                m[("ve_ropeq", c)] = VE["n"] - 1
                m[("ve_ropek", c)] = VE["n"]

                def a10(t, c=c):
                    t.wait_ge(VE["h"], m[("ve_ropeq", c)])
                    for ck in range(8):
                        first, last = ck == 0, ck == 7
                        for hh in range(2):
                            ins = t.matmul(PA[:, hh * 512:(hh + 1) * 512],
                                           xn_fm[:, ck * 128:(ck + 1) * 128],
                                           wv_sb[:, ck * 1024 + hh * 512: ck * 1024 + (hh + 1) * 512],
                                           start=first, stop=last)
                    bump(TE, ins)
                    t.wait_ge(VE["h"], m[("ve_ropek", c)])
                    for ck in range(8):
                        first, last = ck == 0, ck == 7
                        for hh in range(2):
                            ins = t.matmul(PB[:, hh * 512:(hh + 1) * 512],
                                           xn_fm[:, ck * 128:(ck + 1) * 128],
                                           wg_sb[:, ck * 1024 + hh * 512: ck * 1024 + (hh + 1) * 512],
                                           start=first, stop=last)
                    bump(TE, ins)
                on("tensor", a10)
                m[("te_v", c)] = TE["n"] - 1
                m[("te_g", c)] = TE["n"]

                def a11(s, c=c):
                    s.wait_ge(TE["h"], m[("te_v", c)])
                    bump(SC, s.copy(out=v_tm[:, :], in_=PA[:, :]))
                    s.wait_ge(TE["h"], m[("te_g", c)])
                    bump(SC, s.activation(out=gsil_tm[:, :], in_=PB[:, :], func=AF.Silu))
                on("scalar", a11)
                m[("sc_v", c)] = SC["n"] - 1
                m[("sc_g", c)] = SC["n"]

                def a12(t, c=c):
                    t.wait_ge(VE["h"], m[("ve_ropeq", c)])
                    if c >= 1:
                        t.wait_ge(SC["h"], m[("sc_qpkT", c - 1)])
                    for j in range(8):
                        ins = t.transpose(PT2[:, j * 128:(j + 1) * 128],
                                          qt_tm[:, j * 128:(j + 1) * 128], ident_sb[:, :])
                    bump(TE, ins)
                    t.wait_ge(VE["h"], m[("ve_ropek", c)])
                    t.wait_ge(SC["h"], m[("sc_xnfm", c)])
                    for j in range(8):
                        ins = t.transpose(PT1[:, j * 128:(j + 1) * 128],
                                          kt_tm[:, j * 128:(j + 1) * 128], ident_sb[:, :])
                    bump(TE, ins)
                on("tensor", a12)
                m[("te_qT", c)] = TE["n"] - 1
                m[("te_kT", c)] = TE["n"]

                def a13(s, c=c):
                    s.wait_ge(TE["h"], m[("te_qT", c)])
                    bump(SC, s.copy(out=qt_fm[:, :], in_=PT2[:, :]))
                    s.wait_ge(TE["h"], m[("te_kT", c)])
                    bump(SC, s.copy(out=kt_fm[:, :], in_=PT1[:, :]))
                on("scalar", a13)
                m[("sc_qT", c)] = SC["n"] - 1
                m[("sc_kT", c)] = SC["n"]

                for h in range(H):
                    pbase = (h % 2) * 64
                    fmcol = (h // 2) * 128
                    PSh = PS1 if h % 2 == 0 else PS2

                    def a14t(t, c=c, h=h, pbase=pbase, fmcol=fmcol, PSh=PSh):
                        if h == 0:
                            t.wait_ge(SC["h"], m[("sc_qT", c)])
                            t.wait_ge(SC["h"], m[("sc_kT", c)])
                            t.wait_ge(SC["h"], m[("sc_v", c)])
                            if c >= 1:
                                t.wait_ge(VE["h"], m[("ve_state", c - 1)])
                                t.wait_ge(TE["h"], m[("te_smm", c - 1)])
                        if h == 1 and c >= 1:
                            t.wait_ge(VE["h"], m[("ve_idxsb", c - 1)])
                        if h >= 2:
                            t.wait_ge(VE["h"], m[("ve_mask", c, h - 2)])
                        bump(TE, t.matmul(PSh[:, 0:128],
                                          kt_fm[pbase:pbase + 64, fmcol:fmcol + 128],
                                          qt_fm[pbase:pbase + 64, fmcol:fmcol + 128],
                                          start=True, stop=True))
                    on("tensor", a14t)
                    m[("te_score", c, h)] = TE["n"]

                    def a14v(v, c=c, h=h, PSh=PSh):
                        v.wait_ge(TE["h"], m[("te_score", c, h)])
                        bump(VE, v.tensor_tensor(out=sT_sb[h % 2][:, :],
                                                 in0=PSh[:, 0:128], in1=maskT_sb[:, :],
                                                 op=ALU.mult))
                    on("vector", a14v)
                    m[("ve_mask", c, h)] = VE["n"]

                    def a14pv(t, c=c, h=h, pbase=pbase, fmcol=fmcol):
                        if h == 0:
                            t.wait_ge(SC["h"], m[("sc_g", c)])
                        t.wait_ge(VE["h"], m[("ve_mask", c, h)])
                        t.matmul(PA[:, h * 64:(h + 1) * 64], sT_sb[h % 2][:, :],
                                 v_tm[:, h * 64:(h + 1) * 64], start=True, stop=True)
                        bump(TE, t.matmul(PB[:, h * 64:(h + 1) * 64],
                                          qt_fm[pbase:pbase + 64, fmcol:fmcol + 128],
                                          state_bf[pbase:pbase + 64,
                                                   (h // 2) * 64:(h // 2) * 64 + 64],
                                          start=True, stop=True))
                    on("tensor", a14pv)
                    m[("te_pv", c, h)] = TE["n"]

                def a15(t, c=c):
                    t.wait_ge(VE["h"], m[("ve_mask", c, 14)])
                    for h in range(H):
                        ins = t.matmul(PS1[(h % 2) * 64:(h % 2) * 64 + 64,
                                           (h // 2) * 64:(h // 2) * 64 + 64],
                                       kt_tm[:, h * 64:(h + 1) * 64],
                                       v_tm[:, h * 64:(h + 1) * 64],
                                       start=True, stop=True)
                    bump(TE, ins)
                on("tensor", a15)
                m[("te_state", c)] = TE["n"]

                def a15v(v, c=c):
                    v.wait_ge(TE["h"], m[("te_state", c)])
                    v.tensor_tensor(out=stateB[:, :], in0=PS1[:, :], in1=stateA[:, :],
                                    op=ALU.add)
                    v.tensor_tensor(out=stateA[:, :], in0=stateB[:, :],
                                    in1=gammaC_sb[:, :], op=ALU.mult)
                    bump(VE, v.tensor_copy(out=state_bf[:, :], in_=stateA[:, :]))
                on("vector", a15v)
                m[("ve_state", c)] = VE["n"]

                def a16(s, c=c):
                    s.wait_ge(TE["h"], m[("te_pv", c, 15)])
                    bump(SC, s.copy(out=crossf[:, :], in_=PB[:, :]))
                on("scalar", a16)
                m[("sc_crossf", c)] = SC["n"]

                def a16v(v, c=c):
                    v.wait_ge(SC["h"], m[("sc_crossf", c)])
                    v.wait_ge(SC["h"], m[("sc_g", c)])
                    v.tensor_tensor(out=t1f[:, :], in0=PA[:, :], in1=crossf[:, :],
                                    op=ALU.add)
                    bump(VE, v.tensor_tensor(out=og_tm[:, :], in0=t1f[:, :],
                                             in1=gsil_tm[:, :], op=ALU.mult))
                on("vector", a16v)
                m[("ve_og", c)] = VE["n"]

                def a17(t, c=c):
                    t.wait_ge(VE["h"], m[("ve_og", c)])
                    for j in range(8):
                        ins = t.transpose(PT1[:, j * 128:(j + 1) * 128],
                                          og_tm[:, j * 128:(j + 1) * 128], ident_sb[:, :])
                    bump(TE, ins)
                on("tensor", a17)
                m[("te_ogT", c)] = TE["n"]

                def a17s(s, c=c):
                    s.wait_ge(TE["h"], m[("te_ogT", c)])
                    bump(SC, s.copy(out=og_fm[:, :], in_=PT1[:, :]))
                on("scalar", a17s)
                m[("sc_ogT", c)] = SC["n"]

                def a18(t, c=c):
                    t.wait_ge(SC["h"], m[("sc_ogT", c)])
                    t.wait_ge(SC["h"], m[("sc_crossf", c)])
                    for ck in range(8):
                        first, last = ck == 0, ck == 7
                        for hh in range(2):
                            ins = t.matmul(PB[:, hh * 512:(hh + 1) * 512],
                                           og_fm[:, ck * 128:(ck + 1) * 128],
                                           wo_sb[:, ck * 1024 + hh * 512: ck * 1024 + (hh + 1) * 512],
                                           start=first, stop=last)
                    bump(TE, ins)
                on("tensor", a18)
                m[("te_smm", c)] = TE["n"]

                def a18v(v, c=c, par=par):
                    v.wait_ge(TE["h"], m[("te_smm", c)])
                    if c >= 2:
                        v.wait_ge(SPL["h"], m[("spl_s", c - 2)])
                    bump(VE, v.tensor_copy(out=s_bf[par][:, :], in_=PB[:, :]))
                on("vector", a18v)
                m[("ve_sevac", c)] = VE["n"]

                def a18sp(sp, c=c, par=par):
                    sp.wait_ge(VE["h"], m[("ve_sevac", c)])
                    bump(SPL, sp.dma_start(out=s_sp.ap()[c], in_=s_bf[par][:, :]), 16)
                on("sync", a18sp)
                m[("spl_s", c)] = SPL["n"]

                def a19(t, c=c):
                    t.wait_ge(VE["h"], m[("ve_state", c)])
                    for ck in range(8):
                        ins = t.matmul(PS2[:, 0:256],
                                       xn_fm[:, ck * 128:(ck + 1) * 128],
                                       wqu_sb[:, ck * 256:(ck + 1) * 256],
                                       start=(ck == 0), stop=(ck == 7))
                    bump(TE, ins)
                on("tensor", a19)
                m[("te_qpk", c)] = TE["n"]

                def a19s(s, c=c):
                    s.wait_ge(TE["h"], m[("te_qpk", c)])
                    bump(SC, s.copy(out=qpk_sb[:, :], in_=PS2[:, 0:256]))
                on("scalar", a19s)
                m[("sc_qpk", c)] = SC["n"]

                def a19t(t, c=c):
                    t.wait_ge(SC["h"], m[("sc_qpk", c)])
                    for j in range(8):
                        ins = t.transpose(PT2[0:PN, j * 128:(j + 1) * 128],
                                          qpk_sb[:, j * 32:(j + 1) * 32], ident_sb[:, :])
                    bump(TE, ins)
                on("tensor", a19t)
                m[("te_qpkT", c)] = TE["n"]

                def a19s2(s, c=c):
                    s.wait_ge(TE["h"], m[("te_qpkT", c)])
                    bump(SC, s.copy(out=qpk_fm[:, :], in_=PT2[0:PN, :]))
                on("scalar", a19s2)
                m[("sc_qpkT", c)] = SC["n"]

                for j in range(8):
                    PSj = PS1 if j % 2 == 0 else PS2

                    def a20t(t, c=c, j=j, PSj=PSj):
                        if j == 0:
                            t.wait_ge(SC["h"], m[("sc_qpkT", c)])
                        if j >= 2:
                            t.wait_ge(SC["h"], m[("sc_sim", c, j - 2)])
                        bump(TE, t.matmul(PSj[:, 128:256],
                                          qpk_fm[:, j * 128:(j + 1) * 128],
                                          keysT_sb[:, j * 128:(j + 1) * 128],
                                          start=True, stop=True))
                    on("tensor", a20t)
                    m[("te_sim", c, j)] = TE["n"]

                    def a20s(s, c=c, j=j, PSj=PSj):
                        s.wait_ge(TE["h"], m[("te_sim", c, j)])
                        bump(SC, s.copy(out=sim_sb[:, j * 128:(j + 1) * 128],
                                        in_=PSj[:, 128:256]))
                    on("scalar", a20s)
                    m[("sc_sim", c, j)] = SC["n"]

                def a21(v, c=c):
                    v.wait_ge(SC["h"], m[("sc_sim", c, 7)])
                    for j in range(8):
                        v.max(out=sxy[:, j * 8:(j + 1) * 8],
                              in_=sim_sb[:, j * 128:(j + 1) * 128])
                    for j in range(8):
                        for r in range(8):
                            v.scalar_tensor_tensor(
                                out=eqdump[:, :],
                                in0=sim_sb[:, j * 128:(j + 1) * 128],
                                scalar=sxy[:, j * 8 + r:j * 8 + r + 1],
                                in1=iota_sb[:, :],
                                op0=ALU.is_equal, op1=ALU.mult,
                                accum_out=ixy[:, j * 8 + r:j * 8 + r + 1])
                    for h in range(PH):
                        for i in range(8):
                            v.tensor_scalar(
                                out=ascore[:, h * 64 + i * 8: h * 64 + (i + 1) * 8],
                                in0=sxy[:, (PH + h) * 8:(PH + h + 1) * 8],
                                scalar1=sxy[:, h * 8 + i:h * 8 + i + 1],
                                scalar2=None, op0=ALU.add)
                    v.tensor_scalar(out=ixs[:, :], in0=ixy[:, 0:32],
                                    scalar1=128.0, scalar2=None, op0=ALU.mult)
                    v.engine_nop()
                    v.engine_nop()
                    v.engine_nop()
                    v.engine_nop()
                    for h in range(PH):
                        for i in range(8):
                            v.tensor_scalar(
                                out=aidx[:, h * 64 + i * 8: h * 64 + (i + 1) * 8],
                                in0=ixy[:, (PH + h) * 8:(PH + h + 1) * 8],
                                scalar1=ixs[:, h * 8 + i:h * 8 + i + 1],
                                scalar2=None, op0=ALU.add)
                    v.engine_nop()
                    for h in range(PH):
                        v.max(out=scores8[:, h * 8:(h + 1) * 8],
                              in_=ascore[:, h * 64:(h + 1) * 64])
                    v.engine_nop()
                    v.engine_nop()
                    v.engine_nop()
                    v.engine_nop()
                    for h in range(PH):
                        for r in range(8):
                            v.scalar_tensor_tensor(
                                out=eqdump[:, 0:64],
                                in0=ascore[:, h * 64:(h + 1) * 64],
                                scalar=scores8[:, h * 8 + r:h * 8 + r + 1],
                                in1=aidx[:, h * 64:(h + 1) * 64],
                                op0=ALU.is_equal, op1=ALU.mult,
                                accum_out=idx8[:, h * 8 + r:h * 8 + r + 1])
                    v.engine_nop()
                    v.engine_nop()
                    v.engine_nop()
                    v.engine_nop()
                    v.tensor_scalar(out=idx8c[:, :], in0=idx8[:, :], scalar1=16383.0,
                                    scalar2=None, op0=ALU.min)
                    for h in range(PH):
                        v.tensor_scalar(out=negmax[:, h:h + 1],
                                        in0=scores8[:, h * 8:h * 8 + 1],
                                        scalar1=-1.0, scalar2=None, op0=ALU.mult)
                    v.engine_nop()
                    v.engine_nop()
                    v.engine_nop()
                    v.engine_nop()
                    for h in range(PH):
                        v.tensor_scalar(out=wexp[:, h * 8:(h + 1) * 8],
                                        in0=scores8[:, h * 8:(h + 1) * 8],
                                        scalar1=negmax[:, h:h + 1], scalar2=None,
                                        op0=ALU.add)
                    v.engine_nop()
                    v.engine_nop()
                    v.engine_nop()
                    v.engine_nop()
                    v.tensor_scalar(out=exp_x[:, :], in0=wexp[:, :], scalar1=-20.0,
                                    scalar2=1.0 / 16.0, op0=ALU.max, op1=ALU.mult)
                    v.engine_nop()
                    v.engine_nop()
                    v.engine_nop()
                    v.engine_nop()
                    v.tensor_scalar(out=exp_t[:, :], in0=exp_x[:, :], scalar1=1.0 / 12.0,
                                    scalar2=1.0, op0=ALU.mult, op1=ALU.add)
                    v.engine_nop()
                    v.engine_nop()
                    v.engine_nop()
                    v.engine_nop()
                    for k in range(11, 0, -1):
                        v.tensor_tensor(out=exp_u[:, :], in0=exp_x[:, :], in1=exp_t[:, :],
                                        op=ALU.mult)
                        v.engine_nop()
                        v.engine_nop()
                        v.engine_nop()
                        v.engine_nop()
                        v.tensor_scalar(out=exp_t[:, :], in0=exp_u[:, :],
                                        scalar1=1.0 / k, scalar2=1.0,
                                        op0=ALU.mult, op1=ALU.add)
                        v.engine_nop()
                        v.engine_nop()
                        v.engine_nop()
                        v.engine_nop()
                    for _ in range(2):
                        v.tensor_tensor(out=exp_u[:, :], in0=exp_t[:, :],
                                        in1=exp_t[:, :], op=ALU.mult)
                        v.engine_nop()
                        v.engine_nop()
                        v.engine_nop()
                        v.engine_nop()
                        v.tensor_tensor(out=exp_t[:, :], in0=exp_u[:, :],
                                        in1=exp_u[:, :], op=ALU.mult)
                        v.engine_nop()
                        v.engine_nop()
                        v.engine_nop()
                        v.engine_nop()
                    for h in range(PH):
                        v.tensor_scalar(out=d32[:, h * 8:(h + 1) * 8],
                                        in0=exp_t[:, h * 8:(h + 1) * 8], scalar1=1.0,
                                        scalar2=0.0, op0=ALU.mult, op1=ALU.add,
                                        accum_out=denom[:, h:h + 1])
                    v.engine_nop()
                    v.engine_nop()
                    v.engine_nop()
                    v.engine_nop()
                    v.reciprocal(out=recipd[:, :], in_=denom[:, :])
                    v.engine_nop()
                    v.engine_nop()
                    v.engine_nop()
                    v.engine_nop()
                    for h in range(PH):
                        ins = v.tensor_scalar(
                            out=wtil_all[:, c * 32 + h * 8: c * 32 + (h + 1) * 8],
                            in0=exp_t[:, h * 8:(h + 1) * 8],
                            scalar1=recipd[:, h:h + 1], scalar2=None, op0=ALU.mult)
                    bump(VE, ins)
                on("vector", a21)
                m[("ve_topk", c)] = VE["n"]

                def a22(t, c=c):
                    t.wait_ge(VE["h"], m[("ve_topk", c)])
                    t.wait_ge(SC["h"], m[("sc_sim", c, 7)])
                    for ci in range(8):
                        ins = t.matmul(PS2[:, ci * 32:(ci + 1) * 32],
                                       picon_sb[:, ci * 128:(ci + 1) * 128],
                                       idx8c[:, :], start=True, stop=True)
                    bump(TE, ins)
                on("tensor", a22)
                m[("te_perm", c)] = TE["n"]

                def a22v(v, c=c):
                    v.wait_ge(TE["h"], m[("te_perm", c)])
                    if c >= 1:
                        v.wait_ge(SPL["h"], m[("spl_idx", c - 1)])
                    out_ap = bass.AP(idxs_sb, 0, [[256, 128], [8, 32], [1, 8]])
                    in_ap = bass.AP(PS2, 0, [[512, 128], [1, 32], [32, 8]])
                    bump(VE, v.tensor_copy(out=out_ap, in_=in_ap))
                on("vector", a22v)
                m[("ve_idxsb", c)] = VE["n"]

                def a22sp(sp, c=c):
                    sp.wait_ge(VE["h"], m[("ve_idxsb", c)])
                    bump(SPL, sp.dma_start(out=idxs_sp.ap()[c], in_=idxs_sb[:, :]), 16)
                on("sync", a22sp)
                m[("spl_idx", c)] = SPL["n"]

            m["te_endA"] = TE["n"]
            m["ve_endA"] = VE["n"]
            m["sc_endA"] = SC["n"]

        bctx = ExitStack()
        with bctx:
            def bsb(name, p, n, dt):
                return bctx.enter_context(nc.sbuf_tensor(name, [p, n], dt))

            wga_sb = bsb("wga_sb", 128, 8 * 2048, BF16)
            wup_sb = bsb("wup_sb", 128, 8 * 2048, BF16)
            wdo_sb = bsb("wdo_sb", 128, 16 * 1024, BF16)
            xn_tmB = [bsb("xn_tmB0", 128, D, BF16), bsb("xn_tmB1", 128, D, BF16)]
            xn_fmB = [bsb("xn_fmB0", 128, D, BF16), bsb("xn_fmB1", 128, D, BF16)]
            s_bfB = [bsb("s_bfB0", 128, D, BF16), bsb("s_bfB1", 128, D, BF16)]
            xbB = [bsb("xbB0", 128, D, F32), bsb("xbB1", 128, D, F32)]
            idxs_B = [bsb("idxsB0", 128, 256, I16), bsb("idxsB1", 128, 256, I16)]
            gt = [bsb("gt0", 128, 4 * 2048, BF16), bsb("gt1", 128, 4 * 2048, BF16)]
            ewraw = bsb("ewraw", 128, 32, F32)
            ewsil = bsb("ewsil", 128, 32, F32)
            ewf = bsb("ewf", 128, 32, F32)
            diag_rot = [bsb("diag%d" % k, 128, 128, BF16) for k in range(4)]
            g1_bf = bsb("g1_bf", 128, INTER, BF16)
            h1_tm = bsb("h1_tm", 128, INTER, BF16)
            h1_fm = bsb("h1_fm", 128, INTER, BF16)
            ffn_f = bsb("ffn_f", 128, D, F32)
            gsf = bsb("gsf", 128, D, BF16)
            rbuf = bsb("rbuf", 128, D, BF16)
            ostage = [bsb("ostage0", 128, D, F32), bsb("ostage1", 128, D, F32)]

            nidx_reg = [None]

            def b_gloads(g):
                nidx_reg[0] = g.alloc_register("nidxreg")
                g.reg_mov(nidx_reg[0], 512)
                g.wait_ge(TE["h"], m["te_endA"])
                marks = []
                for dst, src, kchunks, n in ((wga_sb, wT["wgaT"], 8, 2048),
                                             (wup_sb, wT["wupT"], 8, 2048),
                                             (wdo_sb, wT["wdoT"], 16, 1024)):
                    if marks:
                        g.wait_ge(LG["h"], marks[-1])
                    src_ap = bass.AP(src.tensor if hasattr(src, 'tensor') else src, 0,
                                     [[n, 128], [128 * n, kchunks], [1, n]])
                    marks.append(bump(LG, g.dma_start(out=dst[:, :], in_=src_ap), 16))
            on("gpsimd", b_gloads)
            m["lg_bw"] = LG["n"]

            if debug:
                def adump(sp):
                    sp.wait_ge(VE["h"], m["ve_endA"])
                    bump(SPL, sp.dma_start(out=dbg_wtilA.ap(), in_=wtil_all[:, :]), 16)
                on("sync", adump)

            for c in range(n_chunks):
                par = c % 2

                def b1(sp, c=c, par=par):
                    if c == 0:
                        sp.wait_ge(TE["h"], m["te_endA"])
                        sp.wait_ge(VE["h"], m["ve_endA"])
                    if c >= 2:
                        sp.wait_ge(VE["h"], m[("ve_out", c - 2)])
                        sp.wait_ge(TE["h"], m[("te_down", c - 2)])
                    sp.wait_ge(SPL["h"], m[("spl_xn", c)])
                    sp.wait_ge(SPL["h"], m[("spl_s", c)])
                    sp.wait_ge(SPL["h"], m[("spl_idx", c)])
                    bump(LSP, sp.dma_start(out=xn_tmB[par][:, :], in_=xn_tm_sp.ap()[c]), 16)
                    bump(LSP, sp.dma_start(out=xn_fmB[par][:, :], in_=xn_fm_sp.ap()[c]), 16)
                    bump(LSP, sp.dma_start(out=s_bfB[par][:, :], in_=s_sp.ap()[c]), 16)
                    bump(LSP, sp.dma_start(out=xbB[par][:, :],
                                           in_=x_in[c * C:(c + 1) * C, :]), 16)
                    bump(LSP, sp.dma_start(out=idxs_B[par][:, :], in_=idxs_sp.ap()[c]), 16)
                on("sync", b1)
                m[("lsp_b1", c)] = LSP["n"]

                for gidx in range(8):
                    def b2(g, c=c, gidx=gidx, par=par):
                        if c == 0 and gidx == 0:
                            g.wait_ge(LG["h"], m["lg_conv"])
                            g.wait_ge(LSP["h"], m[("lsp_b1", c)])
                        if gidx == 0:
                            g.wait_ge(LSP["h"], m[("lsp_b1", c)])
                        i_g = c * 8 + gidx
                        if i_g >= 2:
                            pg = i_g - 2
                            g.wait_ge(VE["h"], m[("ve_ew", pg // 8, pg % 8)])
                            g.wait_ge(TE["h"], m[("te_diag", pg // 8, pg % 8)])
                        if i_g >= 8:
                            ps, pv = m[("ga", (i_g - 8) // 8, (i_g - 8) % 8)]
                            g.wait_ge(GAs[ps]["h"], pv)
                        GAx = GAs[i_g % 8]
                        gt_ap = bass.AP(gt[gidx % 2], 0,
                                        [[4 * 2 * D, 128], [2 * D, 4], [1, 2 * D]])
                        bump(GAx, g.dma_gather(
                            gt_ap, comb_bf.ap(),
                            idxs_B[par][:, gidx * 32:(gidx + 1) * 32],
                            512, nidx_reg[0], 2 * D, elem_step=2 * D), 16)
                    on("gpsimd", b2)
                    m[("ga", c, gidx)] = (c * 8 + gidx) % 8, GAs[(c * 8 + gidx) % 8]["n"]

                    def b3v(v, c=c, gidx=gidx, par=par):
                        gs, gv = m[("ga", c, gidx)]
                        v.wait_ge(GAs[gs]["h"], gv)
                        for j in range(4):
                            ins = v.scalar_tensor_tensor(
                                out=rbuf[:, :],
                                in0=gt[gidx % 2][:, j * 2048:j * 2048 + D],
                                scalar=1.0, in1=xn_tmB[par][:, :],
                                op0=ALU.mult, op1=ALU.mult,
                                accum_out=ewraw[:, gidx * 4 + j:gidx * 4 + j + 1])
                        bump(VE, ins)
                    on("vector", b3v)
                    m[("ve_ew", c, gidx)] = VE["n"]

                    def b3s(s, c=c, gidx=gidx):
                        s.wait_ge(VE["h"], m[("ve_ew", c, gidx)])
                        bump(SC, s.activation(out=ewsil[:, gidx * 4:(gidx + 1) * 4],
                                              in_=ewraw[:, gidx * 4:(gidx + 1) * 4],
                                              func=AF.Silu))
                    on("scalar", b3s)
                    m[("sc_esil", c, gidx)] = SC["n"]

                    def b3v2(v, c=c, gidx=gidx, par=par):
                        v.wait_ge(SC["h"], m[("sc_esil", c, gidx)])
                        if c * 8 + gidx >= 1:
                            pg = c * 8 + gidx - 1
                            v.wait_ge(TE["h"], m[("te_diag", pg // 8, pg % 8)])
                        v.tensor_tensor(out=ewf[:, gidx * 4:(gidx + 1) * 4],
                                        in0=ewsil[:, gidx * 4:(gidx + 1) * 4],
                                        in1=wtil_all[:, c * 32 + gidx * 4:
                                                     c * 32 + (gidx + 1) * 4],
                                        op=ALU.mult)
                        v.engine_nop()
                        v.engine_nop()
                        v.engine_nop()
                        v.engine_nop()
                        for j in range(4):
                            sg = gidx * 4 + j
                            ins = v.tensor_scalar(
                                out=diag_rot[sg % 4][:, :], in0=ident_sb[:, :],
                                scalar1=ewf[:, sg:sg + 1], scalar2=None, op0=ALU.mult)
                        bump(VE, ins)
                    on("vector", b3v2)
                    m[("ve_diag", c, gidx)] = VE["n"]

                    def b3t(t, c=c, gidx=gidx):
                        t.wait_ge(VE["h"], m[("ve_diag", c, gidx)])
                        if gidx == 0:
                            if c >= 1:
                                t.wait_ge(VE["h"], m[("ve_ffn", c - 1)])
                        for j in range(4):
                            sg = gidx * 4 + j
                            t.matmul(PS1[:, :], diag_rot[sg % 4][:, :],
                                     gt[gidx % 2][:, j * 2048 + D:j * 2048 + D + 512],
                                     start=(sg == 0), stop=False)
                            ins = t.matmul(PS2[:, :], diag_rot[sg % 4][:, :],
                                           gt[gidx % 2][:, j * 2048 + D + 512:(j + 1) * 2048],
                                           start=(sg == 0), stop=False)
                        bump(TE, ins)
                    on("tensor", b3t)
                    m[("te_diag", c, gidx)] = TE["n"]

                def b4(t, c=c, par=par):
                    t.wait_ge(LSP["h"], m[("lsp_b1", c)])
                    if c == 0:
                        t.wait_ge(LG["h"], m["lg_bw"])
                    else:
                        t.wait_ge(VE["h"], m[("ve_h1", c - 1)])
                    for ck in range(8):
                        first, last = ck == 0, ck == 7
                        for pc in range(4):
                            PX = PA if pc < 2 else PB
                            ins = t.matmul(
                                PX[:, (pc % 2) * 512:((pc % 2) + 1) * 512],
                                xn_fmB[par][:, ck * 128:(ck + 1) * 128],
                                wga_sb[:, ck * 2048 + pc * 512: ck * 2048 + (pc + 1) * 512],
                                start=first, stop=last)
                    bump(TE, ins)
                on("tensor", b4)
                m[("te_gate", c)] = TE["n"]

                def b4s(s, c=c):
                    s.wait_ge(TE["h"], m[("te_gate", c)])
                    s.activation(out=g1_bf[:, 0:1024], in_=PA[:, :], func=AF.Silu)
                    bump(SC, s.activation(out=g1_bf[:, 1024:2048], in_=PB[:, :],
                                          func=AF.Silu))
                on("scalar", b4s)
                m[("sc_g1", c)] = SC["n"]

                def b5(t, c=c, par=par):
                    t.wait_ge(SC["h"], m[("sc_g1", c)])
                    for ck in range(8):
                        first, last = ck == 0, ck == 7
                        for pc in range(4):
                            PX = PA if pc < 2 else PB
                            ins = t.matmul(
                                PX[:, (pc % 2) * 512:((pc % 2) + 1) * 512],
                                xn_fmB[par][:, ck * 128:(ck + 1) * 128],
                                wup_sb[:, ck * 2048 + pc * 512: ck * 2048 + (pc + 1) * 512],
                                start=first, stop=last)
                    bump(TE, ins)
                on("tensor", b5)
                m[("te_up", c)] = TE["n"]

                def b5v(v, c=c):
                    v.wait_ge(TE["h"], m[("te_up", c)])
                    v.tensor_tensor(out=h1_tm[:, 0:1024], in0=PA[:, :],
                                    in1=g1_bf[:, 0:1024], op=ALU.mult)
                    bump(VE, v.tensor_tensor(out=h1_tm[:, 1024:2048], in0=PB[:, :],
                                             in1=g1_bf[:, 1024:2048], op=ALU.mult))
                on("vector", b5v)
                m[("ve_h1", c)] = VE["n"]

                def b6(t, c=c):
                    t.wait_ge(VE["h"], m[("ve_h1", c)])
                    if c >= 1:
                        t.wait_ge(SC["h"], m[("sc_h1T", c - 1)])
                    for j in range(8):
                        t.transpose(PT1[:, j * 128:(j + 1) * 128],
                                    h1_tm[:, j * 128:(j + 1) * 128], ident_sb[:, :])
                    for j in range(8):
                        ins = t.transpose(PT2[:, j * 128:(j + 1) * 128],
                                          h1_tm[:, 1024 + j * 128:1024 + (j + 1) * 128],
                                          ident_sb[:, :])
                    bump(TE, ins)
                on("tensor", b6)
                m[("te_h1T", c)] = TE["n"]

                def b6s(s, c=c):
                    s.wait_ge(TE["h"], m[("te_h1T", c)])
                    s.copy(out=h1_fm[:, 0:1024], in_=PT1[:, :])
                    bump(SC, s.copy(out=h1_fm[:, 1024:2048], in_=PT2[:, :]))
                on("scalar", b6s)
                m[("sc_h1T", c)] = SC["n"]

                def b7(t, c=c):
                    t.wait_ge(SC["h"], m[("sc_h1T", c)])
                    for ck in range(16):
                        last = ck == 15
                        t.matmul(PS1[:, :], h1_fm[:, ck * 128:(ck + 1) * 128],
                                 wdo_sb[:, ck * 1024:ck * 1024 + 512],
                                 start=False, stop=last)
                        ins = t.matmul(PS2[:, :], h1_fm[:, ck * 128:(ck + 1) * 128],
                                       wdo_sb[:, ck * 1024 + 512:(ck + 1) * 1024],
                                       start=False, stop=last)
                    bump(TE, ins)
                on("tensor", b7)
                m[("te_down", c)] = TE["n"]

                def b8s(s, c=c):
                    s.wait_ge(TE["h"], m[("te_down", c)])
                    if debug:
                        s.copy(out=ffn_f[:, 0:512], in_=PS1[:, :])
                        s.copy(out=ffn_f[:, 512:1024], in_=PS2[:, :])
                    s.activation(out=gsf[:, 0:512], in_=PS1[:, :], func=AF.Silu)
                    bump(SC, s.activation(out=gsf[:, 512:1024], in_=PS2[:, :],
                                          func=AF.Silu))
                on("scalar", b8s)
                m[("sc_gsf", c)] = SC["n"]

                def b8v(v, c=c):
                    v.wait_ge(SC["h"], m[("sc_gsf", c)])
                    bump(VE, v.engine_nop())
                on("vector", b8v)
                m[("ve_ffn", c)] = VE["n"]

                def b9v(v, c=c, par=par):
                    v.wait_ge(SC["h"], m[("sc_gsf", c)])
                    if c >= 2:
                        v.wait_ge(ST["h"], m[("st_out", c - 2)])
                    v.tensor_tensor(out=rbuf[:, :], in0=gsf[:, :], in1=s_bfB[par][:, :],
                                    op=ALU.mult)
                    bump(VE, v.scalar_tensor_tensor(out=ostage[par][:, :],
                                                    in0=rbuf[:, :], scalar=1.0,
                                                    in1=xbB[par][:, :],
                                                    op0=ALU.mult, op1=ALU.add))
                on("vector", b9v)
                m[("ve_out", c)] = VE["n"]

                def b9sp(sp, c=c, par=par):
                    sp.wait_ge(VE["h"], m[("ve_out", c)])
                    bump(ST, sp.dma_start(out=out_d[c * C:(c + 1) * C, :],
                                          in_=ostage[par][:, :]), 16)
                on("sync", b9sp)
                m[("st_out", c)] = ST["n"]

                if debug:
                    def bdbg(sp, c=c, par=par):
                        sp.wait_ge(VE["h"], m[("ve_out", c)])
                        bump(SPL, sp.dma_start(out=dbg_ffn.ap()[c], in_=ffn_f[:, :]), 16)
                        bump(SPL, sp.dma_start(out=dbg_ewraw.ap()[c], in_=ewraw[:, :]), 16)
                        if c == 0:
                            bump(SPL, sp.dma_start(out=dbg_gt.ap(), in_=gt[0][:, :]), 16)
                    on("sync", bdbg)
                    m[("spl_dbg", c)] = SPL["n"]

            if debug:
                def ddbg(sp):
                    sp.wait_ge(VE["h"], m[("ve_out", n_chunks - 1)])
                    bump(SPL, sp.dma_start(out=dbg_wtil.ap(), in_=wtil_all[:, :]), 16)
                    sp.wait_ge(SPL["h"], SPL["n"])
                on("sync", ddbg)

            def fin(sp, nn=n_chunks):
                sp.wait_ge(ST["h"], m[("st_out", nn - 1)])
            on("sync", fin)

            for _s in (LSP, LG, SPL, ST, TE, VE, SC):
                _s["n"] = 0
            for _s in GAs:
                _s["n"] = 0
            mock_mode[0] = False
            for _en in ("sync", "gpsimd", "vector", "scalar", "tensor"):
                _fns = [f for (e, f) in prog if e == _en]

                def _run(eng, _fns=_fns):
                    for f in _fns:
                        f(eng)
                getattr(block, _en)(_run)

    mmask = {}
    for lib in all_libraries:
        for it in lib.instructions:
            mmask[it] = mmask.get(it, 0) | (1 << lib.index)
    _bass_rust.insert_library_loads(nc, mmask, len(all_libraries), standard.index)
    mybir.codegen_inst_isa_subclasses(nc)
    return nc


_NCH_ENV = int(os.environ.get("MK_NCHUNKS", str(NCH)))
_DEBUG_ENV = os.environ.get("MK_DEBUG", "0") == "1"


def kernel(L0_attn_x, L0_attn_wq, L0_attn_wk, L0_attn_wv, L0_attn_wg, L0_attn_wo,
           L0_ffn_w_queries, L0_ffn_keys, L0_ffn_down_embed, L0_ffn_up_embed,
           L0_ffn_w_gate, L0_ffn_w_up, L0_ffn_w_down):
    inp = dict(L0_attn_x=L0_attn_x, L0_attn_wq=L0_attn_wq, L0_attn_wk=L0_attn_wk,
               L0_attn_wv=L0_attn_wv, L0_attn_wg=L0_attn_wg, L0_attn_wo=L0_attn_wo,
               L0_ffn_w_queries=L0_ffn_w_queries, L0_ffn_keys=L0_ffn_keys,
               L0_ffn_down_embed=L0_ffn_down_embed, L0_ffn_up_embed=L0_ffn_up_embed,
               L0_ffn_w_gate=L0_ffn_w_gate, L0_ffn_w_up=L0_ffn_w_up,
               L0_ffn_w_down=L0_ffn_w_down)
    w = _prep_weights(inp)
    cs = _host_consts()
    nch = _NCH_ENV

    nc = build_nc(n_chunks=nch, debug=_DEBUG_ENV)

    base = {
        "wqT": w["wqT"], "wkT": w["wkT"], "wvT": w["wvT"], "wgT": w["wgT"],
        "woT": w["woT"], "wquT": w["wquT"], "wgaT": w["wgaT"], "wupT": w["wupT"],
        "wdoT": w["wdoT"], "keysT_in": w["keysT"],
        "de_in": np.ascontiguousarray(L0_ffn_down_embed),
        "ue_in": np.ascontiguousarray(L0_ffn_up_embed),
        "c_ident": cs["ident_bf"], "c_maskT": cs["maskT_bf"], "c_iota": cs["iota128"],
        "c_cosF": cs["cosF"], "c_sinF": cs["sinF"], "c_gq": cs["gq_bf"],
        "c_gk": cs["gk_bf"], "c_gammaC": cs["gammaC"], "c_picon": cs["picon"],
    }
    in_maps = []
    for b in range(8):
        im = dict(base)
        im["x_in"] = np.ascontiguousarray(L0_attn_x[b])
        in_maps.append(im)

    res = run_bass_kernel_spmd(nc, in_maps, core_ids=list(range(8)))
    out = np.stack([res.results[b]["out"] for b in range(8)], axis=0)
    return out.astype(np.float32)


# version 07 (final)
# speedup vs baseline: 1.1348x; 1.0395x over previous
import os
import math
import numpy as np
import ml_dtypes
from contextlib import ExitStack

import concourse.bass as bass
import concourse.mybir as mybir
import bass_rust as _bass_rust
from concourse.bass_utils import run_bass_kernel_spmd
from concourse.library_config import all_libraries, standard

F32 = mybir.dt.float32
BF16 = mybir.dt.bfloat16
I16 = mybir.dt.int16
AF = mybir.ActivationFunctionType
ALU = mybir.AluOpType

D = 1024
H = 16
DH = 64
C = 128
S = 4096
NCH = S // C
INTER = 2048
NE = 16384
PH = 4
PN = 32
NK = 128
TOPK = 8


def _host_consts():
    cs = {}
    cs["ident_bf"] = np.eye(128, dtype=ml_dtypes.bfloat16)
    cs["maskT_bf"] = np.triu(np.ones((128, 128), dtype=np.float32)).astype(ml_dtypes.bfloat16)
    cs["iota128"] = np.broadcast_to(np.arange(128, dtype=np.float32), (128, 128)).copy()
    inv_freq = 1.0 / (10000.0 ** (np.arange(0, DH, 2, dtype=np.float64) / DH))
    t = np.arange(S, dtype=np.float64)
    fr = np.einsum("i,j->ij", t, inv_freq)
    emb = np.concatenate([fr, fr], axis=-1)
    cosd = np.cos(emb)
    sind = np.sin(emb)
    sneg = sind.copy()
    sneg[:, :PN] = -sind[:, :PN]
    cs["cosF"] = np.tile(cosd, (1, H)).astype(ml_dtypes.bfloat16)
    cs["sinF"] = np.tile(sneg, (1, H)).astype(ml_dtypes.bfloat16)
    gamma = 1.0 - 2.0 ** (-5.0 - np.arange(H, dtype=np.float64))
    a = np.arange(C, dtype=np.float64)
    gq = np.empty((C, D), dtype=np.float64)
    gk = np.empty((C, D), dtype=np.float64)
    for h in range(H):
        gq[:, h * DH:(h + 1) * DH] = (gamma[h] ** (a + 1.0))[:, None] / 8.0
        gk[:, h * DH:(h + 1) * DH] = (gamma[h] ** (-(a + 1.0)))[:, None]
    cs["gq_bf"] = gq.astype(ml_dtypes.bfloat16)
    cs["gk_bf"] = gk.astype(ml_dtypes.bfloat16)
    gc = np.empty((128, 512), dtype=np.float64)
    for h in range(H):
        gc[(h % 2) * 64:(h % 2) * 64 + 64, (h // 2) * 64:(h // 2) * 64 + 64] = gamma[h] ** C
    cs["gammaC"] = gc.astype(np.float32)
    pic = np.zeros((128, 8, 128), dtype=np.float32)
    for c in range(8):
        for m in range(128):
            pic[c * 16 + m % 16, c, m] = 1.0
    cs["picon"] = pic.reshape(128, 8 * 128)
    return cs


def _prep_weights(inp):
    w = {}
    for nm in ("wq", "wk", "wv", "wg", "wo"):
        w[nm + "T"] = np.ascontiguousarray(inp["L0_attn_" + nm].T)
    w["wquT"] = np.ascontiguousarray(inp["L0_ffn_w_queries"].T)
    w["wgaT"] = np.ascontiguousarray(inp["L0_ffn_w_gate"].T)
    w["wupT"] = np.ascontiguousarray(inp["L0_ffn_w_up"].T)
    w["wdoT"] = np.ascontiguousarray(inp["L0_ffn_w_down"].T)
    k = inp["L0_ffn_keys"]
    kt = np.ascontiguousarray(np.transpose(k, (3, 2, 0, 1)).reshape(PN, 2 * PH * NK))
    w["keysT"] = kt
    return w


def build_nc(n_chunks=NCH, debug=False):
    nc = bass.Bass("TRN2", target_bir_lowering=False, dynamic_dma_scratch_size=16384)
    DT = nc.dram_tensor

    x_in = DT("x_in", [S, D], F32, kind="ExternalInput").ap()
    wT = {}
    for nm in ("wqT", "wkT", "wvT", "wgT", "woT"):
        wT[nm] = DT(nm, [D, D], F32, kind="ExternalInput").ap()
    wT["wquT"] = DT("wquT", [D, 256], F32, kind="ExternalInput").ap()
    wT["wgaT"] = DT("wgaT", [D, INTER], F32, kind="ExternalInput").ap()
    wT["wupT"] = DT("wupT", [D, INTER], F32, kind="ExternalInput").ap()
    wT["wdoT"] = DT("wdoT", [INTER, D], F32, kind="ExternalInput").ap()
    keysT_in = DT("keysT_in", [PN, 1024], F32, kind="ExternalInput").ap()
    de_in = DT("de_in", [NE, D], F32, kind="ExternalInput").ap()
    ue_in = DT("ue_in", [NE, D], F32, kind="ExternalInput").ap()
    c_ident = DT("c_ident", [128, 128], BF16, kind="ExternalInput").ap()
    c_maskT = DT("c_maskT", [128, 128], BF16, kind="ExternalInput").ap()
    c_iota = DT("c_iota", [128, 128], F32, kind="ExternalInput").ap()
    c_cosF = DT("c_cosF", [S, D], BF16, kind="ExternalInput").ap()
    c_sinF = DT("c_sinF", [S, D], BF16, kind="ExternalInput").ap()
    c_gq = DT("c_gq", [C, D], BF16, kind="ExternalInput").ap()
    c_gk = DT("c_gk", [C, D], BF16, kind="ExternalInput").ap()
    c_gammaC = DT("c_gammaC", [128, 512], F32, kind="ExternalInput").ap()
    c_picon = DT("c_picon", [128, 8 * 128], F32, kind="ExternalInput").ap()

    out_d = DT("out", [S, D], F32, kind="ExternalOutput").ap()
    spk = "ExternalOutput" if debug else "Internal"
    xn_tm_sp = DT("xn_tm_sp", [NCH, C, D], BF16, kind=spk)
    xn_fm_sp = DT("xn_fm_sp", [NCH, 128, D], BF16, kind=spk)
    s_sp = DT("s_sp", [NCH, C, D], BF16, kind=spk)
    idxs_sp = DT("idxs_sp", [NCH, 128, 256], I16, kind=spk)
    comb_bf = DT("comb_bf", [NE, 2 * D], BF16, kind="Internal")
    if debug:
        dbg_exp = DT("dbg_exp", [NCH, C, D], F32, kind="ExternalOutput")
        dbg_ffn = DT("dbg_ffn", [NCH, C, D], F32, kind="ExternalOutput")
        dbg_wtil = DT("dbg_wtil", [128, NCH * 32], F32, kind="ExternalOutput")
        dbg_ewraw = DT("dbg_ewraw", [NCH, C, 32], F32, kind="ExternalOutput")
        dbg_wtilA = DT("dbg_wtilA", [128, NCH * 32], F32, kind="ExternalOutput")
        dbg_gt = DT("dbg_gt", [128, 4 * 2 * D], BF16, kind="ExternalOutput")

    gctx = ExitStack()
    with gctx:
        def sb(name, p, n, dt):
            return gctx.enter_context(nc.sbuf_tensor(name, [p, n], dt))

        def psum(name, p, n, dt=F32):
            return gctx.enter_context(nc.psum_tensor(name, [p, n], dt))

        def sem(name):
            h = gctx.enter_context(nc.semaphore(name))
            return {"h": h, "n": 0}

        ident_sb = sb("ident_sb", 128, 128, BF16)
        wtil_all = sb("wtil_all", 128, NCH * 32, F32)

        PA = psum("PA", 128, 1024)
        PB = psum("PB", 128, 1024)
        PT1 = psum("PT1", 128, 1024, BF16)
        PT2 = psum("PT2", 128, 1024, BF16)
        PS1 = psum("PS1", 128, 512)
        PS2 = psum("PS2", 128, 512)

        LSP = sem("LSP")
        LG = sem("LG")
        GAs = [sem("GA%d" % k) for k in range(8)]
        SPL = sem("SPL")
        ST = sem("ST")
        TE = sem("TE")
        VE = sem("VE")
        SC = sem("SC")

        block = gctx.enter_context(nc.Block())

        prog = []
        mock_mode = [True]

        class _MockInstr:
            def then_inc(self, *a, **k):
                return self

        class _MockEngine:
            def __getattr__(self, name):
                if name in ("alloc_register",):
                    return lambda *a, **k: None
                return lambda *a, **k: _MockInstr()

        _mock_eng = _MockEngine()

        def on(engname, fn):
            fn(_mock_eng)
            prog.append((engname, fn))

        def bump(s, instr, k=1):
            if not mock_mode[0]:
                instr.then_inc(s["h"], k)
            s["n"] += k
            return s["n"]

        m = {}


        actx = ExitStack()
        with actx:
            def asb(name, p, n, dt):
                return actx.enter_context(nc.sbuf_tensor(name, [p, n], dt))

            maskT_sb = asb("maskT_sb", 128, 128, BF16)
            iota_sb = asb("iota_sb", 128, 128, F32)
            gq_sb = asb("gq_sb", 128, D, BF16)
            gk_sb = asb("gk_sb", 128, D, BF16)
            gammaC_sb = asb("gammaC_sb", 128, 512, F32)
            picon_sb = asb("picon_sb", 128, 8 * 128, F32)
            keysT_sb = asb("keysT_sb", PN, 1024, BF16)
            xb = [asb("xb0", 128, D, F32), asb("xb1", 128, D, F32)]
            cosb = [asb("cosb0", 128, D, BF16), asb("cosb1", 128, D, BF16)]
            sinb = [asb("sinb0", 128, D, BF16), asb("sinb1", 128, D, BF16)]
            stateA = asb("stateA", 128, 512, F32)
            stateB = asb("stateB", 128, 512, F32)
            state_bf = asb("state_bf", 128, 512, BF16)
            wq_sb = asb("wq_sb", 128, 8 * 1024, BF16)
            wk_sb = asb("wk_sb", 128, 8 * 1024, BF16)
            wv_sb = asb("wv_sb", 128, 8 * 1024, BF16)
            wg_sb = asb("wg_sb", 128, 8 * 1024, BF16)
            wo_sb = asb("wo_sb", 128, 8 * 1024, BF16)
            wqu_sb = asb("wqu_sb", 128, 8 * 256, BF16)
            ln_dump = asb("ln_dump", 128, D, BF16)
            sumx = asb("sumx", 128, 1, F32)
            sumsq = asb("sumsq", 128, 1, F32)
            negm = asb("negm", 128, 1, F32)
            vpe = asb("vpe", 128, 1, F32)
            rs_t = asb("rs_t", 128, 1, F32)
            rs_t2 = asb("rs_t2", 128, 1, F32)
            nsc1 = asb("nsc1", 128, 1, F32)
            nsc2 = asb("nsc2", 128, 1, F32)
            xn_tm = asb("xn_tm", 128, D, BF16)
            xn_fm = asb("xn_fm", 128, D, BF16)
            rotq = asb("rotq", 128, D, F32)
            rotk = asb("rotk", 128, D, F32)
            t1f = asb("t1f", 128, D, F32)
            t2f = asb("t2f", 128, D, F32)
            qt_tm = asb("qt_tm", 128, D, BF16)
            kt_tm = asb("kt_tm", 128, D, BF16)
            qt_fm = asb("qt_fm", 128, D, BF16)
            kt_fm = asb("kt_fm", 128, D, BF16)
            v_tm = asb("v_tm", 128, D, BF16)
            gsil_tm = asb("gsil_tm", 128, D, BF16)
            sT_sb = [asb("sT0", 128, 128, BF16), asb("sT1", 128, 128, BF16)]
            crossf = asb("crossf", 128, D, F32)
            og_tm = asb("og_tm", 128, D, BF16)
            og_fm = asb("og_fm", 128, D, BF16)
            s_bf = [asb("s_bf0", 128, D, BF16), asb("s_bf1", 128, D, BF16)]
            qpk_sb = asb("qpk_sb", 128, 256, BF16)
            qpk_fm = asb("qpk_fm", PN, 1024, BF16)
            sim_sb = asb("sim_sb", 128, 1024, F32)
            sxy = asb("sxy", 128, 64, F32)
            ixy = asb("ixy", 128, 64, F32)
            eqdump = asb("eqdump", 128, 128, BF16)
            ascore = asb("ascore", 128, 256, F32)
            aidx = asb("aidx", 128, 256, F32)
            scores8 = asb("scores8", 128, 32, F32)
            idx8 = asb("idx8", 128, 32, F32)
            idx8c = asb("idx8c", 128, 32, F32)
            ixs = asb("ixs", 128, 32, F32)
            exp_x = asb("exp_x", 128, 32, F32)
            negmax = asb("negmax", 128, 4, F32)
            denom = asb("denom", 128, 4, F32)
            recipd = asb("recipd", 128, 4, F32)
            wexp = asb("wexp", 128, 32, F32)
            exp_t = asb("exp_t", 128, 32, F32)
            exp_u = asb("exp_u", 128, 32, F32)
            d32 = asb("d32", 128, 32, F32)
            idxs_sb = asb("idxs_sb", 128, 256, I16)

            def su_sp(sp):
                for dst, src in ((ident_sb, c_ident), (maskT_sb, c_maskT),
                                 (iota_sb, c_iota), (gammaC_sb, c_gammaC),
                                 (picon_sb, c_picon)):
                    bump(LSP, sp.dma_start(out=dst[:, :], in_=src), 16)
            on("sync", su_sp)
            m["lsp_consts"] = LSP["n"]

            def su_v(v):
                v.memset(stateA[:, :], 0.0)
                v.memset(state_bf[:, :], 0.0)
                bump(VE, v.memset(wtil_all[:, :], 0.0))
            on("vector", su_v)
            m["ve_init"] = VE["n"]

            def a_gloads(g):
                marks = []
                for dst, src, kchunks, n in (
                        (wq_sb, wT["wqT"], 8, 1024), (wk_sb, wT["wkT"], 8, 1024),
                        (wv_sb, wT["wvT"], 8, 1024), (wg_sb, wT["wgT"], 8, 1024),
                        (wo_sb, wT["woT"], 8, 1024), (wqu_sb, wT["wquT"], 8, 256)):
                    if len(marks) >= 2:
                        g.wait_ge(LG["h"], marks[-2])
                    src_ap = bass.AP(src.tensor if hasattr(src, 'tensor') else src, 0,
                                     [[n, 128], [128 * n, kchunks], [1, n]])
                    marks.append(bump(LG, g.dma_start(out=dst[:, :], in_=src_ap), 16))
                g.wait_ge(LG["h"], marks[-2])
                bump(LG, g.dma_start(out=keysT_sb[:, :], in_=keysT_in), 16)
                bump(LG, g.dma_start(out=gq_sb[:, :], in_=c_gq), 16)
                bump(LG, g.dma_start(out=gk_sb[:, :], in_=c_gk), 16)
            on("gpsimd", a_gloads)
            m["lg_aw"] = LG["n"]

            def conv_g(g):
                marks = []
                for half, src in ((0, de_in), (1, ue_in)):
                    for k in range(8):
                        r0 = k * (NE // 8)
                        if len(marks) >= 2:
                            g.wait_ge(LG["h"], marks[-2])
                        marks.append(bump(LG, g.dma_start(
                            out=comb_bf.ap()[r0:r0 + NE // 8, half * D:(half + 1) * D],
                            in_=src[r0:r0 + NE // 8, :]), 16))
            on("gpsimd", conv_g)
            m["lg_conv"] = LG["n"]

            for c in range(n_chunks):
                par = c % 2

                def a1(sp, c=c, par=par):
                    if c >= 2:
                        sp.wait_ge(VE["h"], m[("ve_lnapply", c - 2)])
                        sp.wait_ge(VE["h"], m[("ve_ropek", c - 2)])
                    bump(LSP, sp.dma_start(out=xb[par][:, :],
                                           in_=x_in[c * C:(c + 1) * C, :]), 16)
                    bump(LSP, sp.dma_start(out=cosb[par][:, :],
                                           in_=c_cosF[c * C:(c + 1) * C, :]), 16)
                    bump(LSP, sp.dma_start(out=sinb[par][:, :],
                                           in_=c_sinF[c * C:(c + 1) * C, :]), 16)
                on("sync", a1)
                m[("lsp_a1", c)] = LSP["n"]

                def a2(s, c=c, par=par):
                    s.wait_ge(LSP["h"], m[("lsp_a1", c)])
                    if c == 0:
                        s.wait_ge(LSP["h"], m["lsp_consts"])
                        s.wait_ge(LG["h"], m["lg_aw"])
                    s.activation(out=ln_dump[:, :], in_=xb[par][:, :], func=AF.Copy,
                                 accum_out=sumx[:, :])
                    bump(SC, s.activation(out=ln_dump[:, :], in_=xb[par][:, :],
                                          func=AF.Square, accum_out=sumsq[:, :]))
                on("scalar", a2)
                m[("sc_ln", c)] = SC["n"]

                def a3(v, c=c, par=par):
                    v.wait_ge(SC["h"], m[("sc_ln", c)])
                    if c >= 1:
                        v.wait_ge(SPL["h"], m[("spl_xn", c - 1)])
                        v.wait_ge(TE["h"], m[("te_xnT", c - 1)])
                    v.tensor_scalar(out=negm[:, :], in0=sumx[:, :], scalar1=-1.0 / D,
                                    scalar2=None, op0=ALU.mult)
                    v.engine_nop()
                    v.engine_nop()
                    v.engine_nop()
                    v.tensor_tensor(out=nsc1[:, :], in0=negm[:, :], in1=negm[:, :],
                                    op=ALU.mult)
                    v.tensor_scalar(out=nsc2[:, :], in0=sumsq[:, :], scalar1=1.0 / D,
                                    scalar2=1e-5, op0=ALU.mult, op1=ALU.add)
                    v.engine_nop()
                    v.tensor_tensor(out=vpe[:, :], in0=nsc2[:, :], in1=nsc1[:, :],
                                    op=ALU.subtract)
                    v.engine_nop()
                    v.reciprocal(out=nsc1[:, :], in_=vpe[:, :])
                    v.engine_nop()
                    v.engine_nop()
                    v.engine_nop()
                    v.engine_nop()
                    v.engine_nop()
                    v.engine_nop()
                    ra, rb = rs_t, rs_t2
                    v.tensor_scalar(out=ra[:, :], in0=nsc1[:, :], scalar1=0.5,
                                    scalar2=0.5, op0=ALU.mult, op1=ALU.add)
                    v.engine_nop()
                    for _ in range(5):
                        v.tensor_tensor(out=nsc1[:, :], in0=ra[:, :], in1=ra[:, :],
                                        op=ALU.mult)
                        v.engine_nop()
                        v.engine_nop()
                        v.engine_nop()
                        v.tensor_tensor(out=nsc2[:, :], in0=nsc1[:, :], in1=vpe[:, :],
                                        op=ALU.mult)
                        v.engine_nop()
                        v.engine_nop()
                        v.engine_nop()
                        v.tensor_scalar(out=nsc1[:, :], in0=nsc2[:, :], scalar1=-0.5,
                                        scalar2=1.5, op0=ALU.mult, op1=ALU.add)
                        v.engine_nop()
                        v.engine_nop()
                        v.engine_nop()
                        v.tensor_tensor(out=rb[:, :], in0=ra[:, :], in1=nsc1[:, :],
                                        op=ALU.mult)
                        v.engine_nop()
                        v.engine_nop()
                        v.engine_nop()
                        ra, rb = rb, ra
                    bump(VE, v.tensor_scalar(out=xn_tm[:, :], in0=xb[par][:, :],
                                             scalar1=negm[:, :], scalar2=ra[:, :],
                                             op0=ALU.add, op1=ALU.mult))
                on("vector", a3)
                m[("ve_lnapply", c)] = VE["n"]

                def a4(t, c=c):
                    t.wait_ge(VE["h"], m[("ve_lnapply", c)])
                    if c >= 1:
                        t.wait_ge(SC["h"], m[("sc_ogT", c - 1)])
                    for j in range(8):
                        ins = t.transpose(PT1[:, j * 128:(j + 1) * 128],
                                          xn_tm[:, j * 128:(j + 1) * 128], ident_sb[:, :])
                    bump(TE, ins)
                on("tensor", a4)
                m[("te_xnT", c)] = TE["n"]

                def a5(s, c=c):
                    s.wait_ge(TE["h"], m[("te_xnT", c)])
                    if c >= 1:
                        s.wait_ge(TE["h"], m[("te_qpk", c - 1)])
                        s.wait_ge(SPL["h"], m[("spl_xn", c - 1)])
                    bump(SC, s.copy(out=xn_fm[:, :], in_=PT1[:, :]))
                on("scalar", a5)
                m[("sc_xnfm", c)] = SC["n"]

                def a6(sp, c=c):
                    sp.wait_ge(SC["h"], m[("sc_xnfm", c)])
                    bump(SPL, sp.dma_start(out=xn_tm_sp.ap()[c], in_=xn_tm[:, :]), 16)
                    bump(SPL, sp.dma_start(out=xn_fm_sp.ap()[c], in_=xn_fm[:, :]), 16)
                on("sync", a6)
                m[("spl_xn", c)] = SPL["n"]

                def a7(t, c=c):
                    t.wait_ge(SC["h"], m[("sc_xnfm", c)])
                    if c >= 1:
                        t.wait_ge(VE["h"], m[("ve_sevac", c - 1)])
                    for ck in range(8):
                        first, last = ck == 0, ck == 7
                        for hh in range(2):
                            t.matmul(PA[:, hh * 512:(hh + 1) * 512],
                                     xn_fm[:, ck * 128:(ck + 1) * 128],
                                     wq_sb[:, ck * 1024 + hh * 512: ck * 1024 + (hh + 1) * 512],
                                     start=first, stop=last)
                        for hh in range(2):
                            ins = t.matmul(PB[:, hh * 512:(hh + 1) * 512],
                                           xn_fm[:, ck * 128:(ck + 1) * 128],
                                           wk_sb[:, ck * 1024 + hh * 512: ck * 1024 + (hh + 1) * 512],
                                           start=first, stop=last)
                    bump(TE, ins)
                on("tensor", a7)
                m[("te_qk", c)] = TE["n"]

                def a8(s, c=c):
                    s.wait_ge(TE["h"], m[("te_qk", c)])
                    lo_q = bass.AP(rotq, 0, [[D, 128], [DH, H], [1, PN]])
                    hi_q = bass.AP(rotq, PN, [[D, 128], [DH, H], [1, PN]])
                    lo_pa = bass.AP(PA, 0, [[1024, 128], [DH, H], [1, PN]])
                    hi_pa = bass.AP(PA, PN, [[1024, 128], [DH, H], [1, PN]])
                    s.copy(out=lo_q, in_=hi_pa)
                    bump(SC, s.copy(out=hi_q, in_=lo_pa))
                    lo_k = bass.AP(rotk, 0, [[D, 128], [DH, H], [1, PN]])
                    hi_k = bass.AP(rotk, PN, [[D, 128], [DH, H], [1, PN]])
                    lo_pb = bass.AP(PB, 0, [[1024, 128], [DH, H], [1, PN]])
                    hi_pb = bass.AP(PB, PN, [[1024, 128], [DH, H], [1, PN]])
                    s.copy(out=lo_k, in_=hi_pb)
                    bump(SC, s.copy(out=hi_k, in_=lo_pb))
                on("scalar", a8)
                m[("sc_rotq", c)] = SC["n"] - 1
                m[("sc_rotk", c)] = SC["n"]

                def a9(v, c=c, par=par):
                    v.wait_ge(SC["h"], m[("sc_rotq", c)])
                    v.tensor_tensor(out=t1f[:, :], in0=PA[:, :], in1=cosb[par][:, :],
                                    op=ALU.mult)
                    v.tensor_tensor(out=t2f[:, :], in0=rotq[:, :], in1=sinb[par][:, :],
                                    op=ALU.mult)
                    v.tensor_tensor(out=t1f[:, :], in0=t1f[:, :], in1=t2f[:, :],
                                    op=ALU.add)
                    bump(VE, v.tensor_tensor(out=qt_tm[:, :], in0=t1f[:, :],
                                             in1=gq_sb[:, :], op=ALU.mult))
                    v.wait_ge(SC["h"], m[("sc_rotk", c)])
                    v.tensor_tensor(out=t1f[:, :], in0=PB[:, :], in1=cosb[par][:, :],
                                    op=ALU.mult)
                    v.tensor_tensor(out=t2f[:, :], in0=rotk[:, :], in1=sinb[par][:, :],
                                    op=ALU.mult)
                    v.tensor_tensor(out=t1f[:, :], in0=t1f[:, :], in1=t2f[:, :],
                                    op=ALU.add)
                    bump(VE, v.tensor_tensor(out=kt_tm[:, :], in0=t1f[:, :],
                                             in1=gk_sb[:, :], op=ALU.mult))
                on("vector", a9)
                m[("ve_ropeq", c)] = VE["n"] - 1
                m[("ve_ropek", c)] = VE["n"]

                def a10(t, c=c):
                    t.wait_ge(VE["h"], m[("ve_ropeq", c)])
                    for ck in range(8):
                        first, last = ck == 0, ck == 7
                        for hh in range(2):
                            ins = t.matmul(PA[:, hh * 512:(hh + 1) * 512],
                                           xn_fm[:, ck * 128:(ck + 1) * 128],
                                           wv_sb[:, ck * 1024 + hh * 512: ck * 1024 + (hh + 1) * 512],
                                           start=first, stop=last)
                    bump(TE, ins)
                    t.wait_ge(VE["h"], m[("ve_ropek", c)])
                    for ck in range(8):
                        first, last = ck == 0, ck == 7
                        for hh in range(2):
                            ins = t.matmul(PB[:, hh * 512:(hh + 1) * 512],
                                           xn_fm[:, ck * 128:(ck + 1) * 128],
                                           wg_sb[:, ck * 1024 + hh * 512: ck * 1024 + (hh + 1) * 512],
                                           start=first, stop=last)
                    bump(TE, ins)
                on("tensor", a10)
                m[("te_v", c)] = TE["n"] - 1
                m[("te_g", c)] = TE["n"]

                def a11(s, c=c):
                    s.wait_ge(TE["h"], m[("te_v", c)])
                    bump(SC, s.copy(out=v_tm[:, :], in_=PA[:, :]))
                    s.wait_ge(TE["h"], m[("te_g", c)])
                    bump(SC, s.activation(out=gsil_tm[:, :], in_=PB[:, :], func=AF.Silu))
                on("scalar", a11)
                m[("sc_v", c)] = SC["n"] - 1
                m[("sc_g", c)] = SC["n"]

                def a12(t, c=c):
                    t.wait_ge(VE["h"], m[("ve_ropeq", c)])
                    if c >= 1:
                        t.wait_ge(SC["h"], m[("sc_qpkT", c - 1)])
                    for j in range(8):
                        ins = t.transpose(PT2[:, j * 128:(j + 1) * 128],
                                          qt_tm[:, j * 128:(j + 1) * 128], ident_sb[:, :])
                    bump(TE, ins)
                    t.wait_ge(VE["h"], m[("ve_ropek", c)])
                    t.wait_ge(SC["h"], m[("sc_xnfm", c)])
                    for j in range(8):
                        ins = t.transpose(PT1[:, j * 128:(j + 1) * 128],
                                          kt_tm[:, j * 128:(j + 1) * 128], ident_sb[:, :])
                    bump(TE, ins)
                on("tensor", a12)
                m[("te_qT", c)] = TE["n"] - 1
                m[("te_kT", c)] = TE["n"]

                def a13(s, c=c):
                    s.wait_ge(TE["h"], m[("te_qT", c)])
                    bump(SC, s.copy(out=qt_fm[:, :], in_=PT2[:, :]))
                    s.wait_ge(TE["h"], m[("te_kT", c)])
                    bump(SC, s.copy(out=kt_fm[:, :], in_=PT1[:, :]))
                on("scalar", a13)
                m[("sc_qT", c)] = SC["n"] - 1
                m[("sc_kT", c)] = SC["n"]

                for h in range(H):
                    pbase = (h % 2) * 64
                    fmcol = (h // 2) * 128
                    PSh = PS1 if h % 2 == 0 else PS2

                    def a14t(t, c=c, h=h, pbase=pbase, fmcol=fmcol, PSh=PSh):
                        if h == 0:
                            t.wait_ge(SC["h"], m[("sc_qT", c)])
                            t.wait_ge(SC["h"], m[("sc_kT", c)])
                            t.wait_ge(SC["h"], m[("sc_v", c)])
                            if c >= 1:
                                t.wait_ge(VE["h"], m[("ve_state", c - 1)])
                                t.wait_ge(TE["h"], m[("te_smm", c - 1)])
                        if h == 1 and c >= 1:
                            t.wait_ge(VE["h"], m[("ve_idxsb", c - 1)])
                        if h >= 2:
                            t.wait_ge(VE["h"], m[("ve_mask", c, h - 2)])
                        bump(TE, t.matmul(PSh[:, 0:128],
                                          kt_fm[pbase:pbase + 64, fmcol:fmcol + 128],
                                          qt_fm[pbase:pbase + 64, fmcol:fmcol + 128],
                                          start=True, stop=True))
                    on("tensor", a14t)
                    m[("te_score", c, h)] = TE["n"]

                    def a14v(v, c=c, h=h, PSh=PSh):
                        v.wait_ge(TE["h"], m[("te_score", c, h)])
                        bump(VE, v.tensor_tensor(out=sT_sb[h % 2][:, :],
                                                 in0=PSh[:, 0:128], in1=maskT_sb[:, :],
                                                 op=ALU.mult))
                    on("vector", a14v)
                    m[("ve_mask", c, h)] = VE["n"]

                    def a14pv(t, c=c, h=h, pbase=pbase, fmcol=fmcol):
                        if h == 0:
                            t.wait_ge(SC["h"], m[("sc_g", c)])
                        t.wait_ge(VE["h"], m[("ve_mask", c, h)])
                        t.matmul(PA[:, h * 64:(h + 1) * 64], sT_sb[h % 2][:, :],
                                 v_tm[:, h * 64:(h + 1) * 64], start=True, stop=True)
                        bump(TE, t.matmul(PB[:, h * 64:(h + 1) * 64],
                                          qt_fm[pbase:pbase + 64, fmcol:fmcol + 128],
                                          state_bf[pbase:pbase + 64,
                                                   (h // 2) * 64:(h // 2) * 64 + 64],
                                          start=True, stop=True))
                    on("tensor", a14pv)
                    m[("te_pv", c, h)] = TE["n"]

                def a15(t, c=c):
                    t.wait_ge(VE["h"], m[("ve_mask", c, 14)])
                    for h in range(H):
                        ins = t.matmul(PS1[(h % 2) * 64:(h % 2) * 64 + 64,
                                           (h // 2) * 64:(h // 2) * 64 + 64],
                                       kt_tm[:, h * 64:(h + 1) * 64],
                                       v_tm[:, h * 64:(h + 1) * 64],
                                       start=True, stop=True)
                    bump(TE, ins)
                on("tensor", a15)
                m[("te_state", c)] = TE["n"]

                def a15v(v, c=c):
                    v.wait_ge(TE["h"], m[("te_state", c)])
                    v.tensor_tensor(out=stateB[:, :], in0=PS1[:, :], in1=stateA[:, :],
                                    op=ALU.add)
                    v.tensor_tensor(out=stateA[:, :], in0=stateB[:, :],
                                    in1=gammaC_sb[:, :], op=ALU.mult)
                    bump(VE, v.tensor_copy(out=state_bf[:, :], in_=stateA[:, :]))
                on("vector", a15v)
                m[("ve_state", c)] = VE["n"]

                def a16(s, c=c):
                    s.wait_ge(TE["h"], m[("te_pv", c, 15)])
                    bump(SC, s.copy(out=crossf[:, :], in_=PB[:, :]))
                on("scalar", a16)
                m[("sc_crossf", c)] = SC["n"]

                def a16v(v, c=c):
                    v.wait_ge(SC["h"], m[("sc_crossf", c)])
                    v.wait_ge(SC["h"], m[("sc_g", c)])
                    v.tensor_tensor(out=t1f[:, :], in0=PA[:, :], in1=crossf[:, :],
                                    op=ALU.add)
                    bump(VE, v.tensor_tensor(out=og_tm[:, :], in0=t1f[:, :],
                                             in1=gsil_tm[:, :], op=ALU.mult))
                on("vector", a16v)
                m[("ve_og", c)] = VE["n"]

                def a17(t, c=c):
                    t.wait_ge(VE["h"], m[("ve_og", c)])
                    for j in range(8):
                        ins = t.transpose(PT1[:, j * 128:(j + 1) * 128],
                                          og_tm[:, j * 128:(j + 1) * 128], ident_sb[:, :])
                    bump(TE, ins)
                on("tensor", a17)
                m[("te_ogT", c)] = TE["n"]

                def a17s(s, c=c):
                    s.wait_ge(TE["h"], m[("te_ogT", c)])
                    bump(SC, s.copy(out=og_fm[:, :], in_=PT1[:, :]))
                on("scalar", a17s)
                m[("sc_ogT", c)] = SC["n"]

                def a18(t, c=c):
                    t.wait_ge(SC["h"], m[("sc_ogT", c)])
                    t.wait_ge(SC["h"], m[("sc_crossf", c)])
                    for ck in range(8):
                        first, last = ck == 0, ck == 7
                        for hh in range(2):
                            ins = t.matmul(PB[:, hh * 512:(hh + 1) * 512],
                                           og_fm[:, ck * 128:(ck + 1) * 128],
                                           wo_sb[:, ck * 1024 + hh * 512: ck * 1024 + (hh + 1) * 512],
                                           start=first, stop=last)
                    bump(TE, ins)
                on("tensor", a18)
                m[("te_smm", c)] = TE["n"]

                def a18v(v, c=c, par=par):
                    v.wait_ge(TE["h"], m[("te_smm", c)])
                    if c >= 2:
                        v.wait_ge(SPL["h"], m[("spl_s", c - 2)])
                    bump(VE, v.tensor_copy(out=s_bf[par][:, :], in_=PB[:, :]))
                on("vector", a18v)
                m[("ve_sevac", c)] = VE["n"]

                def a18sp(sp, c=c, par=par):
                    sp.wait_ge(VE["h"], m[("ve_sevac", c)])
                    bump(SPL, sp.dma_start(out=s_sp.ap()[c], in_=s_bf[par][:, :]), 16)
                on("sync", a18sp)
                m[("spl_s", c)] = SPL["n"]

                def a19(t, c=c):
                    t.wait_ge(VE["h"], m[("ve_state", c)])
                    for ck in range(8):
                        ins = t.matmul(PS2[:, 0:256],
                                       xn_fm[:, ck * 128:(ck + 1) * 128],
                                       wqu_sb[:, ck * 256:(ck + 1) * 256],
                                       start=(ck == 0), stop=(ck == 7))
                    bump(TE, ins)
                on("tensor", a19)
                m[("te_qpk", c)] = TE["n"]

                def a19s(s, c=c):
                    s.wait_ge(TE["h"], m[("te_qpk", c)])
                    bump(SC, s.copy(out=qpk_sb[:, :], in_=PS2[:, 0:256]))
                on("scalar", a19s)
                m[("sc_qpk", c)] = SC["n"]

                def a19t(t, c=c):
                    t.wait_ge(SC["h"], m[("sc_qpk", c)])
                    for j in range(8):
                        ins = t.transpose(PT2[0:PN, j * 128:(j + 1) * 128],
                                          qpk_sb[:, j * 32:(j + 1) * 32], ident_sb[:, :])
                    bump(TE, ins)
                on("tensor", a19t)
                m[("te_qpkT", c)] = TE["n"]

                def a19s2(s, c=c):
                    s.wait_ge(TE["h"], m[("te_qpkT", c)])
                    bump(SC, s.copy(out=qpk_fm[:, :], in_=PT2[0:PN, :]))
                on("scalar", a19s2)
                m[("sc_qpkT", c)] = SC["n"]

                for j in range(8):
                    PSj = PS1 if j % 2 == 0 else PS2

                    def a20t(t, c=c, j=j, PSj=PSj):
                        if j == 0:
                            t.wait_ge(SC["h"], m[("sc_qpkT", c)])
                        if j >= 2:
                            t.wait_ge(SC["h"], m[("sc_sim", c, j - 2)])
                        bump(TE, t.matmul(PSj[:, 128:256],
                                          qpk_fm[:, j * 128:(j + 1) * 128],
                                          keysT_sb[:, j * 128:(j + 1) * 128],
                                          start=True, stop=True))
                    on("tensor", a20t)
                    m[("te_sim", c, j)] = TE["n"]

                    def a20s(s, c=c, j=j, PSj=PSj):
                        s.wait_ge(TE["h"], m[("te_sim", c, j)])
                        bump(SC, s.copy(out=sim_sb[:, j * 128:(j + 1) * 128],
                                        in_=PSj[:, 128:256]))
                    on("scalar", a20s)
                    m[("sc_sim", c, j)] = SC["n"]

                def a21(v, c=c):
                    v.wait_ge(SC["h"], m[("sc_sim", c, 7)])
                    for j in range(8):
                        v.max(out=sxy[:, j * 8:(j + 1) * 8],
                              in_=sim_sb[:, j * 128:(j + 1) * 128])
                    for j in range(8):
                        for r in range(8):
                            v.scalar_tensor_tensor(
                                out=eqdump[:, :],
                                in0=sim_sb[:, j * 128:(j + 1) * 128],
                                scalar=sxy[:, j * 8 + r:j * 8 + r + 1],
                                in1=iota_sb[:, :],
                                op0=ALU.is_equal, op1=ALU.mult,
                                accum_out=ixy[:, j * 8 + r:j * 8 + r + 1])
                    for h in range(PH):
                        for i in range(8):
                            v.tensor_scalar(
                                out=ascore[:, h * 64 + i * 8: h * 64 + (i + 1) * 8],
                                in0=sxy[:, (PH + h) * 8:(PH + h + 1) * 8],
                                scalar1=sxy[:, h * 8 + i:h * 8 + i + 1],
                                scalar2=None, op0=ALU.add)
                    v.tensor_scalar(out=ixs[:, :], in0=ixy[:, 0:32],
                                    scalar1=128.0, scalar2=None, op0=ALU.mult)
                    v.engine_nop()
                    v.engine_nop()
                    v.engine_nop()
                    v.engine_nop()
                    for h in range(PH):
                        for i in range(8):
                            v.tensor_scalar(
                                out=aidx[:, h * 64 + i * 8: h * 64 + (i + 1) * 8],
                                in0=ixy[:, (PH + h) * 8:(PH + h + 1) * 8],
                                scalar1=ixs[:, h * 8 + i:h * 8 + i + 1],
                                scalar2=None, op0=ALU.add)
                    v.engine_nop()
                    for h in range(PH):
                        v.max(out=scores8[:, h * 8:(h + 1) * 8],
                              in_=ascore[:, h * 64:(h + 1) * 64])
                    v.engine_nop()
                    v.engine_nop()
                    v.engine_nop()
                    v.engine_nop()
                    for h in range(PH):
                        for r in range(8):
                            v.scalar_tensor_tensor(
                                out=eqdump[:, 0:64],
                                in0=ascore[:, h * 64:(h + 1) * 64],
                                scalar=scores8[:, h * 8 + r:h * 8 + r + 1],
                                in1=aidx[:, h * 64:(h + 1) * 64],
                                op0=ALU.is_equal, op1=ALU.mult,
                                accum_out=idx8[:, h * 8 + r:h * 8 + r + 1])
                    v.engine_nop()
                    v.engine_nop()
                    v.engine_nop()
                    v.engine_nop()
                    v.tensor_scalar(out=idx8c[:, :], in0=idx8[:, :], scalar1=16383.0,
                                    scalar2=None, op0=ALU.min)
                    for h in range(PH):
                        v.tensor_scalar(out=negmax[:, h:h + 1],
                                        in0=scores8[:, h * 8:h * 8 + 1],
                                        scalar1=-1.0, scalar2=None, op0=ALU.mult)
                    v.engine_nop()
                    v.engine_nop()
                    v.engine_nop()
                    v.engine_nop()
                    for h in range(PH):
                        v.tensor_scalar(out=wexp[:, h * 8:(h + 1) * 8],
                                        in0=scores8[:, h * 8:(h + 1) * 8],
                                        scalar1=negmax[:, h:h + 1], scalar2=None,
                                        op0=ALU.add)
                    v.engine_nop()
                    v.engine_nop()
                    v.engine_nop()
                    v.engine_nop()
                    v.tensor_scalar(out=exp_x[:, :], in0=wexp[:, :], scalar1=-20.0,
                                    scalar2=1.0 / 16.0, op0=ALU.max, op1=ALU.mult)
                    v.engine_nop()
                    v.engine_nop()
                    v.engine_nop()
                    v.engine_nop()
                    v.tensor_scalar(out=exp_t[:, :], in0=exp_x[:, :], scalar1=1.0 / 12.0,
                                    scalar2=1.0, op0=ALU.mult, op1=ALU.add)
                    v.engine_nop()
                    v.engine_nop()
                    v.engine_nop()
                    v.engine_nop()
                    for k in range(11, 0, -1):
                        v.tensor_tensor(out=exp_u[:, :], in0=exp_x[:, :], in1=exp_t[:, :],
                                        op=ALU.mult)
                        v.engine_nop()
                        v.engine_nop()
                        v.engine_nop()
                        v.engine_nop()
                        v.tensor_scalar(out=exp_t[:, :], in0=exp_u[:, :],
                                        scalar1=1.0 / k, scalar2=1.0,
                                        op0=ALU.mult, op1=ALU.add)
                        v.engine_nop()
                        v.engine_nop()
                        v.engine_nop()
                        v.engine_nop()
                    for _ in range(2):
                        v.tensor_tensor(out=exp_u[:, :], in0=exp_t[:, :],
                                        in1=exp_t[:, :], op=ALU.mult)
                        v.engine_nop()
                        v.engine_nop()
                        v.engine_nop()
                        v.engine_nop()
                        v.tensor_tensor(out=exp_t[:, :], in0=exp_u[:, :],
                                        in1=exp_u[:, :], op=ALU.mult)
                        v.engine_nop()
                        v.engine_nop()
                        v.engine_nop()
                        v.engine_nop()
                    for h in range(PH):
                        v.tensor_scalar(out=d32[:, h * 8:(h + 1) * 8],
                                        in0=exp_t[:, h * 8:(h + 1) * 8], scalar1=1.0,
                                        scalar2=0.0, op0=ALU.mult, op1=ALU.add,
                                        accum_out=denom[:, h:h + 1])
                    v.engine_nop()
                    v.engine_nop()
                    v.engine_nop()
                    v.engine_nop()
                    v.reciprocal(out=recipd[:, :], in_=denom[:, :])
                    v.engine_nop()
                    v.engine_nop()
                    v.engine_nop()
                    v.engine_nop()
                    for h in range(PH):
                        ins = v.tensor_scalar(
                            out=wtil_all[:, c * 32 + h * 8: c * 32 + (h + 1) * 8],
                            in0=exp_t[:, h * 8:(h + 1) * 8],
                            scalar1=recipd[:, h:h + 1], scalar2=None, op0=ALU.mult)
                    bump(VE, ins)
                on("vector", a21)
                m[("ve_topk", c)] = VE["n"]

                def a22(t, c=c):
                    t.wait_ge(VE["h"], m[("ve_topk", c)])
                    t.wait_ge(SC["h"], m[("sc_sim", c, 7)])
                    for ci in range(8):
                        ins = t.matmul(PS2[:, ci * 32:(ci + 1) * 32],
                                       picon_sb[:, ci * 128:(ci + 1) * 128],
                                       idx8c[:, :], start=True, stop=True)
                    bump(TE, ins)
                on("tensor", a22)
                m[("te_perm", c)] = TE["n"]

                def a22v(v, c=c):
                    v.wait_ge(TE["h"], m[("te_perm", c)])
                    if c >= 1:
                        v.wait_ge(SPL["h"], m[("spl_idx", c - 1)])
                    out_ap = bass.AP(idxs_sb, 0, [[256, 128], [8, 32], [1, 8]])
                    in_ap = bass.AP(PS2, 0, [[512, 128], [1, 32], [32, 8]])
                    bump(VE, v.tensor_copy(out=out_ap, in_=in_ap))
                on("vector", a22v)
                m[("ve_idxsb", c)] = VE["n"]

                def a22sp(sp, c=c):
                    sp.wait_ge(VE["h"], m[("ve_idxsb", c)])
                    bump(SPL, sp.dma_start(out=idxs_sp.ap()[c], in_=idxs_sb[:, :]), 16)
                on("sync", a22sp)
                m[("spl_idx", c)] = SPL["n"]

            m["te_endA"] = TE["n"]
            m["ve_endA"] = VE["n"]
            m["sc_endA"] = SC["n"]

        bctx = ExitStack()
        with bctx:
            def bsb(name, p, n, dt):
                return bctx.enter_context(nc.sbuf_tensor(name, [p, n], dt))

            wga_sb = bsb("wga_sb", 128, 8 * 2048, BF16)
            wup_sb = bsb("wup_sb", 128, 8 * 2048, BF16)
            wdo_sb = bsb("wdo_sb", 128, 16 * 1024, BF16)
            xn_tmB = [bsb("xn_tmB0", 128, D, BF16), bsb("xn_tmB1", 128, D, BF16)]
            xn_fmB = [bsb("xn_fmB0", 128, D, BF16), bsb("xn_fmB1", 128, D, BF16)]
            s_bfB = [bsb("s_bfB0", 128, D, BF16), bsb("s_bfB1", 128, D, BF16)]
            xbB = [bsb("xbB0", 128, D, F32), bsb("xbB1", 128, D, F32)]
            idxs_B = [bsb("idxsB0", 128, 256, I16), bsb("idxsB1", 128, 256, I16)]
            gt = [bsb("gt0", 128, 4 * 2048, BF16), bsb("gt1", 128, 4 * 2048, BF16)]
            ewraw = bsb("ewraw", 128, 32, F32)
            ewsil = bsb("ewsil", 128, 32, F32)
            ewf = bsb("ewf", 128, 32, F32)
            diag_rot = [bsb("diag%d" % k, 128, 128, BF16) for k in range(4)]
            g1_bf = bsb("g1_bf", 128, INTER, BF16)
            h1_tm = bsb("h1_tm", 128, INTER, BF16)
            h1_fm = bsb("h1_fm", 128, INTER, BF16)
            ffn_f = bsb("ffn_f", 128, D, F32)
            gsf = bsb("gsf", 128, D, BF16)
            rbuf = bsb("rbuf", 128, D, BF16)
            ostage = [bsb("ostage0", 128, D, F32), bsb("ostage1", 128, D, F32)]

            nidx_reg = [None]

            def b_gloads(g):
                nidx_reg[0] = g.alloc_register("nidxreg")
                g.reg_mov(nidx_reg[0], 512)
                g.wait_ge(TE["h"], m["te_endA"])
                marks = []
                for dst, src, kchunks, n in ((wga_sb, wT["wgaT"], 8, 2048),
                                             (wup_sb, wT["wupT"], 8, 2048),
                                             (wdo_sb, wT["wdoT"], 16, 1024)):
                    if marks:
                        g.wait_ge(LG["h"], marks[-1])
                    src_ap = bass.AP(src.tensor if hasattr(src, 'tensor') else src, 0,
                                     [[n, 128], [128 * n, kchunks], [1, n]])
                    marks.append(bump(LG, g.dma_start(out=dst[:, :], in_=src_ap), 16))
            on("gpsimd", b_gloads)
            m["lg_bw"] = LG["n"]

            if debug:
                def adump(sp):
                    sp.wait_ge(VE["h"], m["ve_endA"])
                    bump(SPL, sp.dma_start(out=dbg_wtilA.ap(), in_=wtil_all[:, :]), 16)
                on("sync", adump)

            for c in range(n_chunks):
                par = c % 2

                def b1(sp, c=c, par=par):
                    if c == 0:
                        sp.wait_ge(TE["h"], m["te_endA"])
                        sp.wait_ge(VE["h"], m["ve_endA"])
                    if c >= 2:
                        sp.wait_ge(VE["h"], m[("ve_out", c - 2)])
                        sp.wait_ge(TE["h"], m[("te_down", c - 2)])
                    sp.wait_ge(SPL["h"], m[("spl_xn", c)])
                    sp.wait_ge(SPL["h"], m[("spl_s", c)])
                    sp.wait_ge(SPL["h"], m[("spl_idx", c)])
                    bump(LSP, sp.dma_start(out=xn_tmB[par][:, :], in_=xn_tm_sp.ap()[c]), 16)
                    bump(LSP, sp.dma_start(out=xn_fmB[par][:, :], in_=xn_fm_sp.ap()[c]), 16)
                    bump(LSP, sp.dma_start(out=s_bfB[par][:, :], in_=s_sp.ap()[c]), 16)
                    bump(LSP, sp.dma_start(out=xbB[par][:, :],
                                           in_=x_in[c * C:(c + 1) * C, :]), 16)
                    bump(LSP, sp.dma_start(out=idxs_B[par][:, :], in_=idxs_sp.ap()[c]), 16)
                on("sync", b1)
                m[("lsp_b1", c)] = LSP["n"]

                def b4(t, c=c, par=par):
                    t.wait_ge(LSP["h"], m[("lsp_b1", c)])
                    if c == 0:
                        t.wait_ge(LG["h"], m["lg_bw"])
                    else:
                        t.wait_ge(VE["h"], m[("ve_h1", c - 1)])
                    for ck in range(8):
                        first, last = ck == 0, ck == 7
                        for pc in range(4):
                            PX = PA if pc < 2 else PB
                            ins = t.matmul(
                                PX[:, (pc % 2) * 512:((pc % 2) + 1) * 512],
                                xn_fmB[par][:, ck * 128:(ck + 1) * 128],
                                wga_sb[:, ck * 2048 + pc * 512: ck * 2048 + (pc + 1) * 512],
                                start=first, stop=last)
                    bump(TE, ins)
                on("tensor", b4)
                m[("te_gate", c)] = TE["n"]

                def b4s(s, c=c):
                    s.wait_ge(TE["h"], m[("te_gate", c)])
                    s.activation(out=g1_bf[:, 0:1024], in_=PA[:, :], func=AF.Silu)
                    bump(SC, s.activation(out=g1_bf[:, 1024:2048], in_=PB[:, :],
                                          func=AF.Silu))
                on("scalar", b4s)
                m[("sc_g1", c)] = SC["n"]

                def b5(t, c=c, par=par):
                    t.wait_ge(SC["h"], m[("sc_g1", c)])
                    for ck in range(8):
                        first, last = ck == 0, ck == 7
                        for pc in range(4):
                            PX = PA if pc < 2 else PB
                            ins = t.matmul(
                                PX[:, (pc % 2) * 512:((pc % 2) + 1) * 512],
                                xn_fmB[par][:, ck * 128:(ck + 1) * 128],
                                wup_sb[:, ck * 2048 + pc * 512: ck * 2048 + (pc + 1) * 512],
                                start=first, stop=last)
                    bump(TE, ins)
                on("tensor", b5)
                m[("te_up", c)] = TE["n"]

                def b5v(v, c=c):
                    v.wait_ge(TE["h"], m[("te_up", c)])
                    v.tensor_tensor(out=h1_tm[:, 0:1024], in0=PA[:, :],
                                    in1=g1_bf[:, 0:1024], op=ALU.mult)
                    bump(VE, v.tensor_tensor(out=h1_tm[:, 1024:2048], in0=PB[:, :],
                                             in1=g1_bf[:, 1024:2048], op=ALU.mult))
                on("vector", b5v)
                m[("ve_h1", c)] = VE["n"]

                for gidx in range(8):
                    def b2(g, c=c, gidx=gidx, par=par):
                        if c == 0 and gidx == 0:
                            g.wait_ge(LG["h"], m["lg_conv"])
                            g.wait_ge(LSP["h"], m[("lsp_b1", c)])
                        if gidx == 0:
                            g.wait_ge(LSP["h"], m[("lsp_b1", c)])
                        i_g = c * 8 + gidx
                        if i_g >= 2:
                            pg = i_g - 2
                            g.wait_ge(VE["h"], m[("ve_ew", pg // 8, pg % 8)])
                            g.wait_ge(TE["h"], m[("te_diag", pg // 8, pg % 8)])
                        if i_g >= 8:
                            ps, pv = m[("ga", (i_g - 8) // 8, (i_g - 8) % 8)]
                            g.wait_ge(GAs[ps]["h"], pv)
                        GAx = GAs[i_g % 8]
                        gt_ap = bass.AP(gt[gidx % 2], 0,
                                        [[4 * 2 * D, 128], [2 * D, 4], [1, 2 * D]])
                        bump(GAx, g.dma_gather(
                            gt_ap, comb_bf.ap(),
                            idxs_B[par][:, gidx * 32:(gidx + 1) * 32],
                            512, nidx_reg[0], 2 * D, elem_step=2 * D), 16)
                    on("gpsimd", b2)
                    m[("ga", c, gidx)] = (c * 8 + gidx) % 8, GAs[(c * 8 + gidx) % 8]["n"]

                    def b3v(v, c=c, gidx=gidx, par=par):
                        gs, gv = m[("ga", c, gidx)]
                        v.wait_ge(GAs[gs]["h"], gv)
                        for j in range(4):
                            ins = v.scalar_tensor_tensor(
                                out=rbuf[:, :],
                                in0=gt[gidx % 2][:, j * 2048:j * 2048 + D],
                                scalar=1.0, in1=xn_tmB[par][:, :],
                                op0=ALU.mult, op1=ALU.mult,
                                accum_out=ewraw[:, gidx * 4 + j:gidx * 4 + j + 1])
                        bump(VE, ins)
                    on("vector", b3v)
                    m[("ve_ew", c, gidx)] = VE["n"]

                    def b3s(s, c=c, gidx=gidx):
                        s.wait_ge(VE["h"], m[("ve_ew", c, gidx)])
                        bump(SC, s.activation(out=ewsil[:, gidx * 4:(gidx + 1) * 4],
                                              in_=ewraw[:, gidx * 4:(gidx + 1) * 4],
                                              func=AF.Silu))
                    on("scalar", b3s)
                    m[("sc_esil", c, gidx)] = SC["n"]

                    def b3v2(v, c=c, gidx=gidx, par=par):
                        v.wait_ge(SC["h"], m[("sc_esil", c, gidx)])
                        if c * 8 + gidx >= 1:
                            pg = c * 8 + gidx - 1
                            v.wait_ge(TE["h"], m[("te_diag", pg // 8, pg % 8)])
                        v.tensor_tensor(out=ewf[:, gidx * 4:(gidx + 1) * 4],
                                        in0=ewsil[:, gidx * 4:(gidx + 1) * 4],
                                        in1=wtil_all[:, c * 32 + gidx * 4:
                                                     c * 32 + (gidx + 1) * 4],
                                        op=ALU.mult)
                        v.engine_nop()
                        v.engine_nop()
                        v.engine_nop()
                        v.engine_nop()
                        for j in range(4):
                            sg = gidx * 4 + j
                            ins = v.tensor_scalar(
                                out=diag_rot[sg % 4][:, :], in0=ident_sb[:, :],
                                scalar1=ewf[:, sg:sg + 1], scalar2=None, op0=ALU.mult)
                        bump(VE, ins)
                    on("vector", b3v2)
                    m[("ve_diag", c, gidx)] = VE["n"]

                    def b3t(t, c=c, gidx=gidx):
                        t.wait_ge(VE["h"], m[("ve_diag", c, gidx)])
                        if gidx == 0:
                            if c >= 1:
                                t.wait_ge(VE["h"], m[("ve_ffn", c - 1)])
                        for j in range(4):
                            sg = gidx * 4 + j
                            t.matmul(PS1[:, :], diag_rot[sg % 4][:, :],
                                     gt[gidx % 2][:, j * 2048 + D:j * 2048 + D + 512],
                                     start=(sg == 0), stop=False)
                            ins = t.matmul(PS2[:, :], diag_rot[sg % 4][:, :],
                                           gt[gidx % 2][:, j * 2048 + D + 512:(j + 1) * 2048],
                                           start=(sg == 0), stop=False)
                        bump(TE, ins)
                    on("tensor", b3t)
                    m[("te_diag", c, gidx)] = TE["n"]

                def b6(t, c=c):
                    t.wait_ge(VE["h"], m[("ve_h1", c)])
                    if c >= 1:
                        t.wait_ge(SC["h"], m[("sc_h1T", c - 1)])
                    for j in range(8):
                        t.transpose(PT1[:, j * 128:(j + 1) * 128],
                                    h1_tm[:, j * 128:(j + 1) * 128], ident_sb[:, :])
                    for j in range(8):
                        ins = t.transpose(PT2[:, j * 128:(j + 1) * 128],
                                          h1_tm[:, 1024 + j * 128:1024 + (j + 1) * 128],
                                          ident_sb[:, :])
                    bump(TE, ins)
                on("tensor", b6)
                m[("te_h1T", c)] = TE["n"]

                def b6s(s, c=c):
                    s.wait_ge(TE["h"], m[("te_h1T", c)])
                    s.copy(out=h1_fm[:, 0:1024], in_=PT1[:, :])
                    bump(SC, s.copy(out=h1_fm[:, 1024:2048], in_=PT2[:, :]))
                on("scalar", b6s)
                m[("sc_h1T", c)] = SC["n"]

                def b7(t, c=c):
                    t.wait_ge(SC["h"], m[("sc_h1T", c)])
                    for ck in range(16):
                        last = ck == 15
                        t.matmul(PS1[:, :], h1_fm[:, ck * 128:(ck + 1) * 128],
                                 wdo_sb[:, ck * 1024:ck * 1024 + 512],
                                 start=False, stop=last)
                        ins = t.matmul(PS2[:, :], h1_fm[:, ck * 128:(ck + 1) * 128],
                                       wdo_sb[:, ck * 1024 + 512:(ck + 1) * 1024],
                                       start=False, stop=last)
                    bump(TE, ins)
                on("tensor", b7)
                m[("te_down", c)] = TE["n"]

                def b8s(s, c=c):
                    s.wait_ge(TE["h"], m[("te_down", c)])
                    if debug:
                        s.copy(out=ffn_f[:, 0:512], in_=PS1[:, :])
                        s.copy(out=ffn_f[:, 512:1024], in_=PS2[:, :])
                    s.activation(out=gsf[:, 0:512], in_=PS1[:, :], func=AF.Silu)
                    bump(SC, s.activation(out=gsf[:, 512:1024], in_=PS2[:, :],
                                          func=AF.Silu))
                on("scalar", b8s)
                m[("sc_gsf", c)] = SC["n"]

                def b8v(v, c=c):
                    v.wait_ge(SC["h"], m[("sc_gsf", c)])
                    bump(VE, v.engine_nop())
                on("vector", b8v)
                m[("ve_ffn", c)] = VE["n"]

                def b9v(v, c=c, par=par):
                    v.wait_ge(SC["h"], m[("sc_gsf", c)])
                    if c >= 2:
                        v.wait_ge(ST["h"], m[("st_out", c - 2)])
                    v.tensor_tensor(out=rbuf[:, :], in0=gsf[:, :], in1=s_bfB[par][:, :],
                                    op=ALU.mult)
                    bump(VE, v.scalar_tensor_tensor(out=ostage[par][:, :],
                                                    in0=rbuf[:, :], scalar=1.0,
                                                    in1=xbB[par][:, :],
                                                    op0=ALU.mult, op1=ALU.add))
                on("vector", b9v)
                m[("ve_out", c)] = VE["n"]

                def b9sp(sp, c=c, par=par):
                    sp.wait_ge(VE["h"], m[("ve_out", c)])
                    bump(ST, sp.dma_start(out=out_d[c * C:(c + 1) * C, :],
                                          in_=ostage[par][:, :]), 16)
                on("sync", b9sp)
                m[("st_out", c)] = ST["n"]

                if debug:
                    def bdbg(sp, c=c, par=par):
                        sp.wait_ge(VE["h"], m[("ve_out", c)])
                        bump(SPL, sp.dma_start(out=dbg_ffn.ap()[c], in_=ffn_f[:, :]), 16)
                        bump(SPL, sp.dma_start(out=dbg_ewraw.ap()[c], in_=ewraw[:, :]), 16)
                        if c == 0:
                            bump(SPL, sp.dma_start(out=dbg_gt.ap(), in_=gt[0][:, :]), 16)
                    on("sync", bdbg)
                    m[("spl_dbg", c)] = SPL["n"]

            if debug:
                def ddbg(sp):
                    sp.wait_ge(VE["h"], m[("ve_out", n_chunks - 1)])
                    bump(SPL, sp.dma_start(out=dbg_wtil.ap(), in_=wtil_all[:, :]), 16)
                    sp.wait_ge(SPL["h"], SPL["n"])
                on("sync", ddbg)

            def fin(sp, nn=n_chunks):
                sp.wait_ge(ST["h"], m[("st_out", nn - 1)])
            on("sync", fin)

            for _s in (LSP, LG, SPL, ST, TE, VE, SC):
                _s["n"] = 0
            for _s in GAs:
                _s["n"] = 0
            mock_mode[0] = False
            for _en in ("sync", "gpsimd", "vector", "scalar", "tensor"):
                _fns = [f for (e, f) in prog if e == _en]

                def _run(eng, _fns=_fns):
                    for f in _fns:
                        f(eng)
                getattr(block, _en)(_run)

    mmask = {}
    for lib in all_libraries:
        for it in lib.instructions:
            mmask[it] = mmask.get(it, 0) | (1 << lib.index)
    _bass_rust.insert_library_loads(nc, mmask, len(all_libraries), standard.index)
    mybir.codegen_inst_isa_subclasses(nc)
    return nc


_NCH_ENV = int(os.environ.get("MK_NCHUNKS", str(NCH)))
_DEBUG_ENV = os.environ.get("MK_DEBUG", "0") == "1"


def kernel(L0_attn_x, L0_attn_wq, L0_attn_wk, L0_attn_wv, L0_attn_wg, L0_attn_wo,
           L0_ffn_w_queries, L0_ffn_keys, L0_ffn_down_embed, L0_ffn_up_embed,
           L0_ffn_w_gate, L0_ffn_w_up, L0_ffn_w_down):
    inp = dict(L0_attn_x=L0_attn_x, L0_attn_wq=L0_attn_wq, L0_attn_wk=L0_attn_wk,
               L0_attn_wv=L0_attn_wv, L0_attn_wg=L0_attn_wg, L0_attn_wo=L0_attn_wo,
               L0_ffn_w_queries=L0_ffn_w_queries, L0_ffn_keys=L0_ffn_keys,
               L0_ffn_down_embed=L0_ffn_down_embed, L0_ffn_up_embed=L0_ffn_up_embed,
               L0_ffn_w_gate=L0_ffn_w_gate, L0_ffn_w_up=L0_ffn_w_up,
               L0_ffn_w_down=L0_ffn_w_down)
    w = _prep_weights(inp)
    cs = _host_consts()
    nch = _NCH_ENV

    nc = build_nc(n_chunks=nch, debug=_DEBUG_ENV)

    base = {
        "wqT": w["wqT"], "wkT": w["wkT"], "wvT": w["wvT"], "wgT": w["wgT"],
        "woT": w["woT"], "wquT": w["wquT"], "wgaT": w["wgaT"], "wupT": w["wupT"],
        "wdoT": w["wdoT"], "keysT_in": w["keysT"],
        "de_in": np.ascontiguousarray(L0_ffn_down_embed),
        "ue_in": np.ascontiguousarray(L0_ffn_up_embed),
        "c_ident": cs["ident_bf"], "c_maskT": cs["maskT_bf"], "c_iota": cs["iota128"],
        "c_cosF": cs["cosF"], "c_sinF": cs["sinF"], "c_gq": cs["gq_bf"],
        "c_gk": cs["gk_bf"], "c_gammaC": cs["gammaC"], "c_picon": cs["picon"],
    }
    in_maps = []
    for b in range(8):
        im = dict(base)
        im["x_in"] = np.ascontiguousarray(L0_attn_x[b])
        in_maps.append(im)

    res = run_bass_kernel_spmd(nc, in_maps, core_ids=list(range(8)))
    out = np.stack([res.results[b]["out"] for b in range(8)], axis=0)
    return out.astype(np.float32)
